# version 10
# baseline (speedup 1.0000x reference)
import contextlib
import numpy as np
import concourse.bass as bass
import concourse.mybir as mybir
from concourse.alu_op_type import AluOpType as ALU
from concourse.bass_utils import run_bass_kernel_spmd

F32 = mybir.dt.float32
BF16 = mybir.dt.bfloat16
AF = mybir.ActivationFunctionType
AX = mybir.AxisListType.X

T = 2048
D = 1024
NT = T // 128
EPS = 1e-5


class Sem:
    __slots__ = ("h", "v", "_pb", "_slot")
    def __init__(self, h, pb=None, slot=None):
        self.h = h
        self.v = 0
        self._pb = pb
        self._slot = slot

    @property
    def b(self):
        return self._pb[self._slot] if self._pb is not None else 0


def WG(eng, sem, v):
    eng.wait_ge(sem.h, sem.b + v)


class Prog:
    ENGS = ("sync", "gpsimd", "vector", "scalar", "tensor")

    def __init__(self, nc, es):
        self.nc = nc
        self.es = es
        self.gq = []
        self.pool = None
        self.pool_base = []
        self._pc = 0
        self.pref = ""
        self.block_sems = None
        self.block_cms = None
        self.semmap = {}
        self.chains = {"vector": self.sem("chv"), "scalar": self.sem("chs"),
                       "gpsimd": self.sem("chg")}
        self.last = {"vector": None, "scalar": None, "gpsimd": None}

    def sem(self, name, dma=False):
        if self.pool is not None and self.block_sems is not None:
            if dma:
                self._pcd -= 1
                slot = self._pcd
            else:
                slot = self._pc
                self._pc += 1
            assert self._pc <= self._pcd, "sem pool exhausted"
            h = self.pool[slot]
            self.semmap.setdefault(self.pref, {})[f"sp{slot}"] = name
            sm = Sem(h, self.pool_base, slot)
            self.block_sems.append(sm)
            return sm
        sm = Sem(self.es.enter_context(self.nc.semaphore(self.pref + name)))
        if self.block_sems is not None:
            self.block_sems.append(sm)
        return sm

    def begin_block(self, pref):
        self.pref = pref
        self._pc = 0
        self._pcd = len(self.pool) if self.pool is not None else 0
        self.block_sems = []
        self.block_cms = []
        self.last = {"vector": None, "scalar": None, "gpsimd": None}

    def end_block(self):
        sems = self.block_sems
        for cm in reversed(self.block_cms):
            cm.__exit__(None, None, None)
        self.block_sems = None
        self.block_cms = None
        self.pref = ""
        return sems

    def on(self, engname):
        def deco(f):
            self.gq.append((engname, f))
            return None
        return deco

    def on_if(self, cond, engname):
        if cond:
            return self.on(engname)
        def deco(f):
            return None
        return deco

    def sb(self, name, shape, dt=F32):
        cm = self.nc.sbuf_tensor(self.pref + name, shape, dt)
        if self.block_cms is not None:
            h = cm.__enter__()
            self.block_cms.append(cm)
            return h
        return self.es.enter_context(cm)

    def ps(self, name, shape, dt=F32):
        cm = self.nc.psum_tensor(self.pref + name, shape, dt)
        if self.block_cms is not None:
            h = cm.__enter__()
            self.block_cms.append(cm)
            return h
        return self.es.enter_context(cm)

    def ps2(self, name, shape, dt=F32):
        cm = self.nc.psum_tensor(self.pref + name, shape, dt)
        return cm.__enter__(), cm

    def sb2(self, name, shape, dt=F32):
        cm = self.nc.sbuf_tensor(self.pref + name, shape, dt)
        return cm.__enter__(), cm

    def chain(self, engname, eng, fname, *a, inc=None, **kw):
        last = self.last[engname]
        if last is not None:
            eng.wait_ge(last[0], last[1])
        ins = getattr(eng, fname)(*a, **kw)
        s = inc if inc is not None else self.chains[engname]
        s.v += 1
        ins.then_inc(s.h, 1)
        self.last[engname] = (s.h, s.b + s.v)
        return ins

    def vop(self, v, fname, *a, inc=None, **kw):
        return self.chain("vector", v, fname, *a, inc=inc, **kw)

    def sop(self, s, fname, *a, inc=None, **kw):
        return self.chain("scalar", s, fname, *a, inc=inc, **kw)

    def gop(self, g, fname, *a, inc=None, **kw):
        return self.chain("gpsimd", g, fname, *a, inc=inc, **kw)

    def finalize(self):
        nc = self.nc
        gq = self.gq
        with nc.Block() as block:
            def runner(_eng, gq=gq, nc=nc):
                for engname, f in gq:
                    f(getattr(nc, engname))
            block.sync(runner)


def emit_barrier(p, k, blocksems, bar0, bar1):
    @p.on("sync")
    def _(sy):
        for s in blocksems:
            if s.v:
                WG(sy, s, s.v)
        p.nc.all_engine_barrier()
        sy.sem_inc(bar1.h, 1)

    for e in ("gpsimd", "vector", "scalar", "tensor"):
        @p.on(e)
        def _(eng):
            WG(eng, bar1, k + 1)

    @p.on("sync")
    def _(sy):
        for s in blocksems:
            if s._pb is not None:
                s._pb[s._slot] += s.v
        p.last["vector"] = None
        p.last["scalar"] = None
        p.last["gpsimd"] = None


def emit_ln(p, sx, gate, pref, n_tiles=NT, sq_buf=None):
    sq = sq_buf if sq_buf is not None else p.sb(pref + "sq", [128, D])
    s1 = p.sb(pref + "s1", [128, n_tiles])
    s2 = p.sb(pref + "s2", [128, n_tiles])
    mall = p.sb(pref + "m", [128, n_tiles])
    var = p.sb(pref + "var", [128, n_tiles])
    var2 = p.sb(pref + "var2", [128, n_tiles])
    msq = p.sb(pref + "msq", [128, n_tiles])
    lntmp = p.sb(pref + "lntmp", [128, n_tiles])
    rstd = p.sb(pref + "rstd", [128, n_tiles])
    negm = p.sb(pref + "negm", [128, n_tiles])
    s_stat = p.sem(pref + "stat")
    s_var = p.sem(pref + "var")
    s_rstd = p.sem(pref + "rstd")
    gh, gv = gate

    @p.on("scalar")
    def _(s):
        WG(s, gh, gv)
        for t in range(n_tiles):
            p.sop(s, "activation", sq[:, 0:D], sx[:, t * D:(t + 1) * D], AF.Square,
                  accum_out=s2[:, t:t + 1], inc=(s_stat if t == n_tiles - 1 else None))
        WG(s, s_var, 1)
        p.sop(s, "activation", lntmp[:, :], var2[:, :], AF.Ln)
        p.sop(s, "activation", rstd[:, :], lntmp[:, :], AF.Exp, scale=-0.5, inc=s_rstd)

    @p.on("vector")
    def _(v):
        WG(v, gh, gv)
        for t in range(n_tiles):
            p.vop(v, "reduce_sum", s1[:, t:t + 1], sx[:, t * D:(t + 1) * D], AX)
        WG(v, s_stat, 1)
        p.vop(v, "tensor_scalar", mall[:, :], s1[:, :], 1.0 / D, None, ALU.mult)
        p.vop(v, "tensor_scalar", var[:, :], s2[:, :], 1.0 / D, EPS, ALU.mult, ALU.add)
        p.vop(v, "tensor_tensor", msq[:, :], mall[:, :], mall[:, :], ALU.mult)
        p.vop(v, "tensor_tensor", var2[:, :], var[:, :], msq[:, :], ALU.subtract, inc=s_var)
        p.vop(v, "tensor_scalar", negm[:, :], mall[:, :], -1.0, None, ALU.mult)
    return rstd, negm, s_rstd



def emit_moe(p, A, sx, gate):
    rw = A["rw_dev"]
    rb = A["rb"]
    ew = A["ew_dev"]
    eb = A["eb_row"]
    ident = A["ident"]
    xlnT = p.sb("xlnT", [128, NT * D], BF16)
    xln = p.sb("xln0", [128, D], BF16)
    probs = p.sb("probs", [128, 8 * NT])
    sumall = p.sb("sumall", [128, NT])
    ghalf = p.sb("ghalf", [128, NT])
    ghalf2 = p.sb("ghalf2", [128, NT])
    negM = p.sb("negM", [128, NT])
    selall = p.sb("selall", [128, 8 * NT])
    scales = p.sb("scales", [128, 8 * NT])
    amb = p.sb("amb", [128, 8])
    csb = p.sb("csb", [128, 8])
    ones8 = p.sb("ones8", [128, 8])
    ones_bf = p.sb("ones_bf", [1, 128], BF16)
    rw_sb = p.sb("rw_sb", [128, 64], BF16)
    rb_sb = p.sb("rb_sb", [1, 8], BF16)
    eb_sb = p.sb("eb_sb", [1, 8 * D], BF16)
    id_bf = p.sb("id_bf", [128, 128], BF16)
    w_sb = [p.sb(f"w{i}", [128, 8 * D], BF16) for i in range(2)]

    ptr = p.ps("ptr", [128, D], BF16)
    pr = [p.ps(f"pr{i}", [128, 8]) for i in range(2)]
    py = [p.ps(f"py{i}", [128, D]) for i in range(2)]

    psem = p.sem("psem", dma=True)
    s_idl = p.sem("sidl", dma=True)
    wload = [p.sem("wl0", dma=True), p.sem("wl1", dma=True)]
    s_init = p.sem("init")
    s_norm = p.sem("nrm")
    s_tp = p.sem("tp")
    s_ev = p.sem("ev")
    s_rt = p.sem("rt")
    s_nm = p.sem("nm")
    s_ex = p.sem("ex")
    s_sel = p.sem("sel")
    s_scl = p.sem("scl")
    s_mm = p.sem("mm")
    s_stt = p.sem("stt")

    @p.on("gpsimd")
    def _(g):
        g.dma_start(out=id_bf[:, :], in_=ident).then_inc(s_idl.h, 16)
        g.dma_start(out=rw_sb[:, :], in_=rw).then_inc(psem.h, 16)
        g.dma_start(out=rb_sb[:, :], in_=rb).then_inc(psem.h, 16)
        g.dma_start(out=eb_sb[:, :], in_=eb).then_inc(psem.h, 16)
        for e in range(8):
            if e >= 2:
                WG(g, s_mm, 16 * (e - 1))
            g.dma_start(out=w_sb[e % 2][:, :], in_=ew[e, :, :]).then_inc(wload[e % 2].h, 16)

    rstd, negm, s_rstd = emit_ln(p, sx, gate, "ln")

    @p.on("scalar")
    def _(s):
        for t in range(NT):
            WG(s, s_nm, t + 1)
            p.sop(s, "activation", probs[:, 8 * t:8 * (t + 1)], pr[t % 2][:, :], AF.Exp,
                  bias=negM[:, t:t + 1], accum_out=sumall[:, t:t + 1], inc=s_ex)

    @p.on("tensor")
    def _(te):
        WG(te, s_init, 1)
        WG(te, s_idl, 16)
        WG(te, psem, 48)
        for t in range(NT):
            WG(te, s_norm, t + 1)
            if t >= 1:
                WG(te, s_ev, t)
            for fc in range(8):
                ins = te.transpose(ptr[:, fc * 128:(fc + 1) * 128],
                                   xln[:, fc * 128:(fc + 1) * 128], id_bf[:, :])
            ins.then_inc(s_tp.h, 1)
            WG(te, s_ev, t + 1)
            if t >= 2:
                WG(te, s_nm, t - 1)
                WG(te, s_ex, t - 1)
            for kc in range(8):
                te.matmul(pr[t % 2][:, :],
                          xlnT[:, (t * 8 + kc) * 128:(t * 8 + kc + 1) * 128],
                          rw_sb[:, kc * 8:(kc + 1) * 8],
                          start=(kc == 0), stop=(kc == 7))
            te.matmul(pr[t % 2][:, :], ones_bf[:, :], rb_sb[:, :],
                      start=False, stop=True,
                      skip_group_check=True).then_inc(s_rt.h, 1)
    s_tp.v += NT
    s_rt.v += NT

    @p.on("vector")
    def _(v):
        p.vop(v, "memset", ones8[:, :], 1.0)
        p.vop(v, "memset", ones_bf[:, :], 1.0, inc=s_init)
        WG(v, s_rstd, 1)
        for t in range(NT):
            if t >= 1:
                WG(v, s_tp, t)
            p.vop(v, "tensor_scalar", xln[:, :], sx[:, t * D:(t + 1) * D],
                  negm[:, t:t + 1], rstd[:, t:t + 1], ALU.add, ALU.mult, inc=s_norm)
            WG(v, s_tp, t + 1)
            p.vop(v, "tensor_copy", xlnT[:, t * D:(t + 1) * D], ptr[:, :], inc=s_ev)
            WG(v, s_rt, t + 1)
            p.vop(v, "tensor_reduce", negM[:, t:t + 1], pr[t % 2][:, :], AX, ALU.max,
                  negate=True, inc=s_nm)
        for t in range(NT):
            WG(v, s_ex, t + 1)
            p.vop(v, "tensor_scalar", amb[:, :], probs[:, 8 * t:8 * (t + 1)], 1.0,
                  None, ALU.is_ge)
            p.vop(v, "tensor_tensor_scan", csb[:, :], ones8[:, :], amb[:, :], 0.0,
                  ALU.mult, ALU.add)
            p.vop(v, "scalar_tensor_tensor", selall[:, 8 * t:8 * (t + 1)], csb[:, :],
                  1.0, amb[:, :], ALU.is_equal, ALU.mult, inc=s_sel)
        p.vop(v, "reciprocal", ghalf[:, :], sumall[:, :])
        p.vop(v, "tensor_scalar", ghalf2[:, :], ghalf[:, :], 0.5, None, ALU.mult)
        for t in range(NT):
            p.vop(v, "tensor_scalar", scales[:, 8 * t:8 * (t + 1)],
                  selall[:, 8 * t:8 * (t + 1)], ghalf2[:, t:t + 1], None, ALU.mult,
                  inc=(s_scl if t == NT - 1 else None))

    @p.on("tensor")
    def _(te):
        for e in range(8):
            WG(te, wload[e % 2], 16 * (e // 2 + 1))
            for t in range(NT):
                k = e * NT + t
                if k >= 2:
                    WG(te, s_stt, k - 1)
                for kc in range(8):
                    for h in range(2):
                        te.matmul(py[k % 2][:, h * 512:(h + 1) * 512],
                                  xlnT[:, (t * 8 + kc) * 128:(t * 8 + kc + 1) * 128],
                                  w_sb[e % 2][:, kc * D + h * 512:kc * D + (h + 1) * 512],
                                  start=(kc == 0), stop=(kc == 7))
                te.matmul(py[k % 2][:, 0:512], ones_bf[:, :],
                          eb_sb[:, e * D:e * D + 512], start=False, stop=True,
                          skip_group_check=True)
                te.matmul(py[k % 2][:, 512:1024], ones_bf[:, :],
                          eb_sb[:, e * D + 512:(e + 1) * D],
                          start=False, stop=True,
                          skip_group_check=True).then_inc(s_mm.h, 1)
    s_mm.v += 8 * NT

    @p.on("vector")
    def _(v):
        WG(v, s_scl, 1)
        for e in range(8):
            for t in range(NT):
                k = e * NT + t
                WG(v, s_mm, k + 1)
                p.vop(v, "scalar_tensor_tensor", sx[:, t * D:(t + 1) * D],
                      py[k % 2][:, :], scales[:, t * 8 + e:t * 8 + e + 1],
                      sx[:, t * D:(t + 1) * D], ALU.mult, ALU.add, inc=s_stt)

    s_idl.v = 16
    psem.v = 48
    wload[0].v = 64
    wload[1].v = 64
    return s_stt, 8 * NT


def moe_aps(nc):
    return {
        "rw_dev": nc.dram_tensor("rw_dev", [128, 64], F32, kind="ExternalInput").ap(),
        "rb": nc.dram_tensor("rb", [1, 8], F32, kind="ExternalInput").ap(),
        "ew_dev": nc.dram_tensor("ew_dev", [8, 128, 8 * D], F32, kind="ExternalInput").ap(),
        "eb_row": nc.dram_tensor("eb_row", [1, 8 * D], F32, kind="ExternalInput").ap(),
    }


def _standalone(emitter, aps_fn, stage_extra=None):
    nc = bass.Bass("TRN2", target_bir_lowering=False)
    x = nc.dram_tensor("x", [T, D], F32, kind="ExternalInput").ap()
    xo = nc.dram_tensor("xo", [T, D], F32, kind="ExternalOutput").ap()
    A = aps_fn(nc)
    A["ident"] = nc.dram_tensor("ident", [128, 128], F32, kind="ExternalInput").ap()
    with contextlib.ExitStack() as es:
        p = Prog(nc, es)
        sx = p.sb("sx", [128, NT * D])
        gdsem = p.sem("gdsem")

        @p.on("sync")
        def _(sy):
            for t in range(NT):
                sy.dma_start(out=sx[:, t * D:(t + 1) * D],
                             in_=x[t * 128:(t + 1) * 128, :]).then_inc(gdsem.h, 16)

        p.begin_block("b0")
        done, dtotal = emitter(p, A, sx, (gdsem, 16 * NT))
        p.end_block()

        @p.on("sync")
        def _(sy):
            WG(sy, done, dtotal)
            for t in range(NT):
                sy.dma_start(out=xo[t * 128:(t + 1) * 128, :],
                             in_=sx[:, t * D:(t + 1) * D]).then_inc(gdsem.h, 16)
            WG(sy, gdsem, 16 * NT + 16 * NT)

        p.finalize()
    return nc


def build_moe(dbg=False):
    return _standalone(emit_moe, moe_aps)



def emit_mamba(p, A, sx, gate, phases=3, p2dt=16):
    NDT = 16
    dbg = False
    dbgt = {}
    w_in = A["w_in_dev"]
    w_cv = A["w_conv_dev"]
    w_xp = A["w_xproj_dev"]
    w_dt = A["w_dt_dev"]
    alog = A["alog_dev"]
    bdt = A["bdt_dev"]
    dd = A["dd_dev"]
    w_out = A["w_out_dev"]
    seld = A["sel_dev"]
    ident = A["ident"]
    hs_scr = A["hs_scr"]
    yt_scr = A["yt_scr"]
    xlnT = p.sb("xlnT", [128, 8 * T], BF16)
    xln = p.sb("xln0", [128, D], BF16)
    big = [p.sb(f"big{i}", [128, 2048]) for i in range(7)]
    big7 = p.sb("big7", [128, 2048], BF16)
    dAb = [None, None]
    xdbl_bf = p.sb("xdbl_bf", [128, 2048], BF16)
    shs = [p.sb(f"shs{i}", [128, 2048], BF16) for i in range(2)]
    dbxb2 = p.sb("dbxb2", [128, 2048], BF16)
    syt = [p.sb(f"syt{i}", [128, 2048], BF16) for i in range(2)]
    win_sb = [p.sb(f"win{i}", [128, 1024], BF16) for i in range(2)]
    wres_sb = [p.sb(f"wres{i}", [128, 1024], BF16) for i in range(2)]
    wout_sb = [p.sb(f"wout{i}", [128, 2048], BF16) for i in range(2)]
    wxp_sb = [p.sb(f"wxp{i}", [128, 128], BF16) for i in range(2)]

    wcv_sb = p.sb("wcv_sb", [128, 64])
    sel_sb = p.sb("sel_sb", [128, 2048], BF16)
    wdt_sb = sel_sb
    alog_sb = p.sb("alog_sb", [128, 256])
    aneg_sb = alog_sb
    bdt_sb = p.sb("bdt_sb", [128, 16])
    dd_sb = p.sb("dd_sb", [128, 16])
    id_bf = p.sb("id_bf", [128, 128], BF16)

    psem = p.sem("psem", dma=True)
    s_id = p.sem("sid", dma=True)
    s_wl = [p.sem("wl0", dma=True), p.sem("wl1", dma=True)]
    s_xl = [p.sem("xl0", dma=True), p.sem("xl1", dma=True)]
    s_wr = [p.sem("wr0", dma=True), p.sem("wr1", dma=True)]
    s_wo = [p.sem("wo0", dma=True), p.sem("wo1", dma=True)]
    s_hso = [p.sem("hso0", dma=True), p.sem("hso1", dma=True)]
    s_hsi = [p.sem("hsi0", dma=True), p.sem("hsi1", dma=True)]
    s_yto = [p.sem("yto0", dma=True), p.sem("yto1", dma=True)]
    s_yti = [p.sem(f"yti{i}", dma=True) for i in range(2)]
    s_n0 = p.sem("n0")
    s_tp0 = p.sem("tp0")
    s_ev0 = p.sem("ev0")
    s_hmm = p.sem("hmm")
    s_conv = p.sem("conv")
    s_silu1 = p.sem("silu1")
    s_xpj = p.sem("xpj")
    s_xadd = p.sem("xadd")
    s_xbf = p.sem("xbf")
    s_Aexp = p.sem("Aexp")
    s_Aneg = p.sem("Aneg")
    s_dtp = p.sem("dtp")
    s_pacc = p.sem("pacc")
    s_dts = p.sem("dts")
    s_delta = p.sem("delta")
    s_dA = p.sem("dA")
    s_scan = p.sem("scan")
    s_repB = p.sem("repB")
    s_repC = p.sem("repC")
    s_dbx = p.sem("dbx")
    s_prod = p.sem("prod")
    s_res = p.sem("res")
    s_silu = p.sem("silu")
    s_ep = p.sem("ep")
    s_ommf = p.sem("ommf")
    s_oevc = p.sem("oevc")
    s_otp = p.sem("otp")
    s_ores = p.sem("ores")

    ptr0, ptr0_cm = p.ps2("ptr0", [128, D], BF16)

    @p.on("gpsimd")
    def _(g):
        g.dma_start(out=id_bf[:, :], in_=ident).then_inc(s_id.h, 16)
        g.dma_start(out=wdt_sb[64:128, :], in_=w_dt).then_inc(psem.h, 16)
        g.dma_start(out=sel_sb[0:48, :], in_=seld).then_inc(psem.h, 16)
        g.dma_start(out=alog_sb[:, :], in_=alog).then_inc(psem.h, 16)
        g.dma_start(out=bdt_sb[:, :], in_=bdt).then_inc(psem.h, 16)
        g.dma_start(out=dd_sb[:, :], in_=dd).then_inc(psem.h, 16)
        g.dma_start(out=wcv_sb[:, :].rearrange("p (k c) -> p k c", k=16),
                    in_=w_cv.rearrange("k p c -> p k c")).then_inc(psem.h, 16)
        for dt in range(NDT if phases >= 1 else 0):
            if dt >= 2:
                WG(g, s_hmm, dt - 1)
            g.dma_start(out=win_sb[dt % 2][:, :], in_=w_in[dt, :, :]).then_inc(s_wl[dt % 2].h, 16)
            if dt >= 2:
                WG(g, s_xpj, dt - 1)
            g.dma_start(out=wxp_sb[dt % 2][:, :], in_=w_xp[dt, :, :]).then_inc(s_xl[dt % 2].h, 16)


    rstd, negm, s_rstd = emit_ln(p, sx, gate, "ln", sq_buf=big[0])

    @p.on("vector")
    def _(v):
        WG(v, s_rstd, 1)
        for t in range(NT):
            if t >= 1:
                WG(v, s_tp0, t)
            p.vop(v, "tensor_scalar", xln[:, :], sx[:, t * D:(t + 1) * D],
                  negm[:, t:t + 1], rstd[:, t:t + 1], ALU.add, ALU.mult, inc=s_n0)
            WG(v, s_tp0, t + 1)
            for fc in range(8):
                p.vop(v, "tensor_copy",
                      xlnT[:, fc * T + t * 128:fc * T + (t + 1) * 128],
                      ptr0[:, fc * 128:(fc + 1) * 128],
                      inc=(s_ev0 if fc == 7 else None))

    @p.on("tensor")
    def _(te):
        WG(te, s_id, 16)
        for t in range(NT):
            WG(te, s_n0, t + 1)
            if t >= 1:
                WG(te, s_ev0, t)
            for fc in range(8):
                ins = te.transpose(ptr0[:, fc * 128:(fc + 1) * 128],
                                   xln[:, fc * 128:(fc + 1) * 128], id_bf[:, :])
            ins.then_inc(s_tp0.h, 1)
    s_tp0.v += NT

    ptr0_cm.__exit__(None, None, None)
    psum_h, psum_h_cm = p.ps2("psum_h", [128, 2048])
    psum_x, psum_x_cm = p.ps2("psum_x", [128, 2048])

    @p.on_if(phases >= 1, "tensor")
    def _(te):
        WG(te, s_ev0, NT)
        WG(te, psem, 96)
        for dt in range(NDT):
            WG(te, s_wl[dt % 2], 16 * (dt // 2 + 1))
            if dt >= 1:
                WG(te, s_conv, dt)
            for nh in range(4):
                for kc in range(8):
                    ins = te.matmul(psum_h[:, nh * 512:(nh + 1) * 512],
                                    win_sb[dt % 2][:, kc * 128:(kc + 1) * 128],
                                    xlnT[:, kc * T + nh * 512:kc * T + (nh + 1) * 512],
                                    start=(kc == 0), stop=(kc == 7))
            ins.then_inc(s_hmm.h, 1)
            WG(te, s_silu1, dt + 1)
            WG(te, s_xl[dt % 2], 16 * (dt // 2 + 1))
            if dt >= 1:
                WG(te, s_xadd, dt)
            for nh in range(4):
                ins = te.matmul(psum_x[:, nh * 512:(nh + 1) * 512],
                                wxp_sb[dt % 2][:, :],
                                shs[dt % 2][:, nh * 512:(nh + 1) * 512],
                                start=True, stop=True)
            ins.then_inc(s_xpj.h, 1)
    s_hmm.v += NDT
    s_xpj.v += NDT

    @p.on_if(phases >= 1, "vector")
    def _(v):
        for dt in range(NDT):
            WG(v, s_hmm, dt + 1)
            if dt >= 1:
                WG(v, s_silu1, dt)
            p.vop(v, "tensor_scalar", big[0][:, :], psum_h[:, :],
                  wcv_sb[:, dt * 4 + 3:dt * 4 + 4], None, ALU.mult)
            for k in (2, 1, 0):
                sft = 3 - k
                p.vop(v, "scalar_tensor_tensor", big[0][:, sft:2048],
                      psum_h[:, 0:2048 - sft], wcv_sb[:, dt * 4 + k:dt * 4 + k + 1],
                      big[0][:, sft:2048], ALU.mult, ALU.add,
                      inc=(s_conv if k == 0 else None))
            WG(v, s_xpj, dt + 1)
            if dt == 0:
                p.vop(v, "tensor_scalar", big[6][:, :], psum_x[:, :], 1.0, None,
                      ALU.mult, inc=s_xadd)
            else:
                p.vop(v, "scalar_tensor_tensor", big[6][:, :], psum_x[:, :], 1.0,
                      big[6][:, :], ALU.mult, ALU.add, inc=s_xadd)
        p.vop(v, "tensor_scalar", xdbl_bf[:, :], big[6][:, :], 1.0, None, ALU.mult,
              inc=s_xbf)
        WG(v, s_Aexp, 1)
        p.vop(v, "tensor_scalar", aneg_sb[:, :], alog_sb[:, :], -1.0, None, ALU.mult,
              inc=s_Aneg)

    @p.on_if(phases >= 1, "scalar")
    def _(s):
        for dt in range(NDT):
            WG(s, s_conv, dt + 1)
            if dt >= 2:
                WG(s, s_xpj, dt - 1)
                WG(s, s_hso[dt % 2], 16 * (dt // 2))
            p.sop(s, "activation", shs[dt % 2][:, :], big[0][:, :], AF.Silu,
                  inc=s_silu1)
        WG(s, psem, 96)
        p.sop(s, "activation", alog_sb[:, :], alog_sb[:, :], AF.Exp, inc=s_Aexp)

    @p.on_if(phases >= 1, "sync")
    def _(sy):
        for dt in range(NDT):
            WG(sy, s_silu1, dt + 1)
            sy.dma_start(out=hs_scr[dt, :, :], in_=shs[dt % 2][:, :]).then_inc(s_hso[dt % 2].h, 16)

    psum_x_cm.__exit__(None, None, None)
    psum_h_cm.__exit__(None, None, None)
    psum_d, psum_d_cm = p.ps2("psum_d", [128, 2048])
    psum_r, psum_r_cm = p.ps2("psum_r", [128, 2048])
    prr = [psum_r, psum_d]
    dAb[0] = big[3]
    dAb[1] = big[6]

    @p.on_if(phases >= 2, "sync")
    def _(sy):
        WG(sy, s_hso[0], 16 * 8)
        WG(sy, s_hso[1], 16 * 8)
        WG(sy, s_xpj, 16)
        for dt in range(p2dt):
            if dt >= 2:
                WG(sy, s_ep, dt - 1)
            sy.dma_start(out=shs[dt % 2][:, :], in_=hs_scr[dt, :, :]).then_inc(s_hsi[dt % 2].h, 16)
            if dt >= 1:
                WG(sy, s_ep, dt)
                sy.dma_start(out=yt_scr[dt - 1, :, :],
                             in_=syt[(dt - 1) % 2][:, :]).then_inc(s_yto[(dt - 1) % 2].h, 16)
        WG(sy, s_ep, p2dt)
        sy.dma_start(out=yt_scr[p2dt - 1, :, :],
                     in_=syt[(p2dt - 1) % 2][:, :]).then_inc(s_yto[(p2dt - 1) % 2].h, 16)

    @p.on_if(phases >= 2, "tensor")
    def _(te):
        WG(te, s_xbf, 1)
        for dt in range(p2dt):
            if dt >= 1:
                WG(te, s_silu, dt)
                WG(te, s_prod, 16 * dt)
            for nh in range(4):
                ins = te.matmul(psum_d[:, nh * 512:(nh + 1) * 512],
                                wdt_sb[64:128, dt * 128:(dt + 1) * 128],
                                xdbl_bf[64:128, nh * 512:(nh + 1) * 512],
                                start=True, stop=True)
            ins.then_inc(s_dtp.h, 1)
            for n in range(16):
                k = dt * 16 + n
                b = prr[n % 2]
                if n == 1:
                    WG(te, s_dts, dt + 1)
                if n >= 2:
                    WG(te, s_prod, k - 1)
                for nh in range(4):
                    ins = te.matmul(b[:, nh * 512:(nh + 1) * 512],
                                    sel_sb[0:16, n * 128:(n + 1) * 128],
                                    xdbl_bf[0:16, nh * 512:(nh + 1) * 512],
                                    start=True, stop=True)
                ins.then_inc(s_repB.h, 1)
                WG(te, s_dbx, k + 1)
                for nh in range(4):
                    ins = te.matmul(b[:, nh * 512:(nh + 1) * 512],
                                    sel_sb[32:48, n * 128:(n + 1) * 128],
                                    xdbl_bf[32:48, nh * 512:(nh + 1) * 512],
                                    start=True, stop=True)
                ins.then_inc(s_repC.h, 1)
            WG(te, s_dts, dt + 1)
            WG(te, s_prod, 16 * (dt + 1))
            WG(te, s_wr[dt % 2], 16 * (dt // 2 + 1))
            for nh in range(4):
                for kc in range(8):
                    ins = te.matmul(psum_d[:, nh * 512:(nh + 1) * 512],
                                    wres_sb[dt % 2][:, kc * 128:(kc + 1) * 128],
                                    xlnT[:, kc * T + nh * 512:kc * T + (nh + 1) * 512],
                                    start=(kc == 0), stop=(kc == 7))
            ins.then_inc(s_res.h, 1)

    @p.on_if(phases >= 2, "vector")
    def _(v):
        for dt in range(p2dt):
            WG(v, s_dtp, dt + 1)
            if dt >= 1:
                WG(v, s_delta, dt)
            p.vop(v, "tensor_scalar", big[0][:, :], psum_d[:, :],
                  bdt_sb[:, dt:dt + 1], 80.0, ALU.add, ALU.min, inc=s_dts)
            WG(v, s_delta, dt + 1)
            WG(v, s_hsi[dt % 2], 16 * (dt // 2 + 1))
            p.vop(v, "tensor_tensor", big[2][:, :], big[1][:, :], shs[dt % 2][:, :],
                  ALU.mult)
            for n in range(16):
                k = dt * 16 + n
                WG(v, s_repB, k + 1)
                p.vop(v, "tensor_tensor", dbxb2[:, :], big[2][:, :],
                      prr[n % 2][:, :], ALU.mult, inc=s_dbx)
                WG(v, s_dA, k + 1)
                p.vop(v, "tensor_tensor_scan", big[4][:, :], dAb[k % 2][:, :],
                      dbxb2[:, :], 0.0, ALU.mult, ALU.add, inc=s_scan)
                WG(v, s_repC, k + 1)
                if n == 0:
                    WG(v, s_pacc, 15 * dt)
                    p.vop(v, "tensor_tensor", big[5][:, :], big[4][:, :],
                          prr[n % 2][:, :], ALU.mult, inc=s_prod)
                else:
                    WG(v, s_pacc, 15 * dt + n - 1)
                    p.vop(v, "tensor_tensor", big7[:, :], big[4][:, :],
                          prr[n % 2][:, :], ALU.mult, inc=s_prod)
            WG(v, s_pacc, 15 * (dt + 1))
            p.vop(v, "scalar_tensor_tensor", big[2][:, :], shs[dt % 2][:, :],
                  dd_sb[:, dt:dt + 1], big[5][:, :], ALU.mult, ALU.add)
            WG(v, s_silu, dt + 1)
            WG(v, s_yto[dt % 2], 16 * (dt // 2))
            p.vop(v, "tensor_tensor", syt[dt % 2][:, :], big[2][:, :],
                  syt[dt % 2][:, :], ALU.mult, inc=s_ep)

    @p.on_if(phases >= 2, "gpsimd")
    def _(g):
        for dt in range(p2dt):
            if dt >= 2:
                WG(g, s_res, dt - 1)
            g.dma_start(out=wres_sb[dt % 2][:, :],
                        in_=w_in[16 + dt, :, :]).then_inc(s_wr[dt % 2].h, 16)
            for n in range(1, 16):
                k = dt * 16 + n
                WG(g, s_prod, k + 1)
                p.gop(g, "tensor_tensor", big[5][:, :], big[5][:, :],
                      big7[:, :], ALU.add, inc=s_pacc)
            if phases >= 3 and dt in (12, 14):
                m = (dt - 12) // 2
                g.dma_start(out=wout_sb[m % 2][:, :],
                            in_=w_out[m, :, :]).then_inc(s_wo[m % 2].h, 16)

    @p.on_if(phases >= 3, "gpsimd")
    def _(g):
        for m in range(2, 8):
            WG(g, s_ommf, NDT * (m - 1))
            g.dma_start(out=wout_sb[m % 2][:, :],
                        in_=w_out[m, :, :]).then_inc(s_wo[m % 2].h, 16)

    @p.on_if(phases >= 2, "scalar")
    def _(s):
        WG(s, s_Aneg, 1)
        for dt in range(p2dt):
            WG(s, s_dts, dt + 1)
            if dt >= 1:
                WG(s, s_ep, dt)
            p.sop(s, "activation", big[0][:, :], big[0][:, :], AF.Exp)
            p.sop(s, "activation", big[1][:, :], big[0][:, :], AF.Ln, bias=1.0,
                  inc=s_delta)
            for n in range(16):
                k = dt * 16 + n
                if k >= 1:
                    WG(s, s_scan, k - 1)
                p.sop(s, "activation", dAb[k % 2][:, :], big[1][:, :], AF.Exp,
                      scale=aneg_sb[:, dt * 16 + n:dt * 16 + n + 1], inc=s_dA)
            WG(s, s_res, dt + 1)
            WG(s, s_yto[dt % 2], 16 * (dt // 2))
            p.sop(s, "activation", syt[dt % 2][:, :], psum_d[:, :], AF.Silu,
                  inc=s_silu)

    psum_r_cm.__exit__(None, None, None)
    psum_d_cm.__exit__(None, None, None)
    psum_o, psum_o_cm = p.ps2("psum_o", [128, 2048])
    ptr2, ptr2_cm = p.ps2("ptr2", [128, 2048], BF16)
    syt4 = syt
    outT_sb = shs[0]

    @p.on_if(phases >= 3, "sync")
    def _(sy):
        WG(sy, s_yto[0], 16 * 8)
        WG(sy, s_yto[1], 16 * 8)
        for m in range(8):
            for dt in range(NDT):
                j = m * NDT + dt
                sl = j % 2
                if j >= 2:
                    WG(sy, s_ommf, j - 1)
                sy.dma_start(out=syt4[sl][:, :], in_=yt_scr[dt, :, :]).then_inc(s_yti[sl].h, 16)

    @p.on_if(phases >= 3, "tensor")
    def _(te):
        for m in range(8):
            WG(te, s_wo[m % 2], 16 * (m // 2 + 1))
            if m >= 1:
                WG(te, s_oevc, m)
            for dt in range(NDT):
                j = m * NDT + dt
                sl = j % 2
                WG(te, s_yti[sl], 16 * (j // 2 + 1))
                for nh in range(4):
                    ins = te.matmul(psum_o[:, nh * 512:(nh + 1) * 512],
                                    wout_sb[m % 2][:, dt * 128:(dt + 1) * 128],
                                    syt4[sl][:, nh * 512:(nh + 1) * 512],
                                    start=(dt == 0), stop=(dt == NDT - 1))
                ins.then_inc(s_ommf.h, 1)
            WG(te, s_oevc, m + 1)
            if m >= 1:
                WG(te, s_ores, 16 * m)
            for t in range(NT):
                ins = te.transpose(ptr2[:, t * 128:(t + 1) * 128],
                                   outT_sb[:, t * 128:(t + 1) * 128], id_bf[:, :])
            ins.then_inc(s_otp.h, 1)

    @p.on_if(phases >= 3, "scalar")
    def _(s):
        for m in range(8):
            WG(s, s_ommf, NDT * (m + 1))
            if m >= 1:
                WG(s, s_otp, m)
            if m == 0:
                WG(s, s_ep, NDT)
            p.sop(s, "activation", outT_sb[:, :], psum_o[:, :], AF.Copy, inc=s_oevc)

    @p.on_if(phases >= 3, "vector")
    def _(v):
        for m in range(8):
            WG(v, s_otp, m + 1)
            for t in range(NT):
                p.vop(v, "scalar_tensor_tensor",
                      sx[:, t * D + m * 128:t * D + (m + 1) * 128],
                      ptr2[:, t * 128:(t + 1) * 128], 1.0,
                      sx[:, t * D + m * 128:t * D + (m + 1) * 128],
                      ALU.mult, ALU.add, inc=s_ores)


    ptr2_cm.__exit__(None, None, None)
    psum_o_cm.__exit__(None, None, None)
    s_id.v = 16
    psem.v = 96
    for q in (s_wl, s_xl, s_wr, s_hso, s_hsi, s_yto):
        q[0].v = 128
        q[1].v = 128
    s_wo[0].v = 64
    s_wo[1].v = 64
    s_yti[0].v = 1024
    s_yti[1].v = 1024
    s_dtp.v = 16
    s_repB.v = 256
    s_repC.v = 256
    s_res.v = 16
    s_ommf.v = 128
    s_otp.v = 8
    return s_ores, 8 * NT


def mamba_aps(nc):
    return {
        "w_in_dev": nc.dram_tensor("w_in_dev", [32, 128, 1024], F32, kind="ExternalInput").ap(),
        "w_conv_dev": nc.dram_tensor("w_conv_dev", [16, 128, 4], F32, kind="ExternalInput").ap(),
        "w_xproj_dev": nc.dram_tensor("w_xproj_dev", [16, 128, 128], F32, kind="ExternalInput").ap(),
        "w_dt_dev": nc.dram_tensor("w_dt_dev", [64, 2048], F32, kind="ExternalInput").ap(),
        "alog_dev": nc.dram_tensor("alog_dev", [128, 256], F32, kind="ExternalInput").ap(),
        "bdt_dev": nc.dram_tensor("bdt_dev", [128, 16], F32, kind="ExternalInput").ap(),
        "dd_dev": nc.dram_tensor("dd_dev", [128, 16], F32, kind="ExternalInput").ap(),
        "w_out_dev": nc.dram_tensor("w_out_dev", [8, 128, 2048], F32, kind="ExternalInput").ap(),
        "sel_dev": nc.dram_tensor("sel_dev", [48, 2048], F32, kind="ExternalInput").ap(),
        "hs_scr": nc.dram_tensor("hs_scr", [16, 128, 2048], BF16).ap(),
        "yt_scr": nc.dram_tensor("yt_scr", [16, 128, 2048], BF16).ap(),
    }


def build_mamba(dbg=False, phases=3, p2dt=16):
    return _standalone(emit_mamba, mamba_aps)



def _stage_moe(inputs):
    rw = inputs["L0_ffn_router_w"]
    rb = inputs["L0_ffn_router_b"]
    ew = inputs["L0_ffn_expert_w"]
    eb = inputs["L0_ffn_expert_b"]
    rw_dev = np.ascontiguousarray(
        rw.reshape(8, 128, 8).transpose(1, 0, 2).reshape(128, 64))
    ew_dev = np.ascontiguousarray(
        ew.reshape(8, 8, 128, D).transpose(0, 2, 1, 3).reshape(8, 128, 8 * D))
    return {
        "rw_dev": rw_dev,
        "rb": np.ascontiguousarray(rb.reshape(1, 8)),
        "ew_dev": ew_dev,
        "eb_row": np.ascontiguousarray(eb.reshape(1, 8 * D)),
        "ident": np.eye(128, dtype=np.float32),
    }


def _stage_mamba(inputs):
    w_in = inputs["L0_attn_w_in"]
    w_conv = inputs["L0_attn_w_conv"]
    w_xproj = inputs["L0_attn_w_xproj"]
    w_dtw = inputs["L0_attn_w_dt"]
    b_dt = inputs["L0_attn_b_dt"]
    a_log = inputs["L0_attn_A_log"]
    dvec = inputs["L0_attn_D"]
    w_outw = inputs["L0_attn_w_out"]
    c = np.ascontiguousarray
    w_in_dev = c(w_in.reshape(32, 128, 8, 128).transpose(0, 3, 2, 1).reshape(32, 128, 1024))
    w_conv_dev = c(w_conv.reshape(16, 128, 4))
    wxT = w_xproj.T
    wx = np.zeros((2048, 128), np.float32)
    wx[:, 0:16] = wxT[:, 64:80]
    wx[:, 32:48] = wxT[:, 80:96]
    wx[:, 64:128] = wxT[:, 0:64]
    w_xproj_dev = c(wx.reshape(16, 128, 128))
    w_dt_dev = c(w_dtw.T)
    alog_dev = c(a_log.reshape(16, 128, 16).transpose(1, 0, 2).reshape(128, 256))
    bdt_dev = c(b_dt.reshape(16, 128).T.reshape(128, 16))
    dd_dev = c(dvec.reshape(16, 128).T.reshape(128, 16))
    w_out_dev = c(w_outw.T.reshape(16, 128, 8, 128).transpose(2, 1, 0, 3).reshape(8, 128, 2048))
    sel1 = np.zeros((16, 16, 128), np.float32)
    for n in range(16):
        sel1[n, n, :] = 1.0
    sel1 = sel1.transpose(1, 0, 2).reshape(16, 2048)
    sel_dev = np.zeros((48, 2048), np.float32)
    sel_dev[0:16] = sel1
    sel_dev[32:48] = sel1
    sel_dev = c(sel_dev)
    return {
        "w_in_dev": w_in_dev, "w_conv_dev": w_conv_dev, "w_xproj_dev": w_xproj_dev,
        "w_dt_dev": w_dt_dev, "alog_dev": alog_dev, "bdt_dev": bdt_dev,
        "dd_dev": dd_dev, "w_out_dev": w_out_dev, "sel_dev": sel_dev,
        "ident": np.eye(128, dtype=np.float32),
    }



def emit_attn(p, A, sx, gate):
    NC = 16
    SCALE = 1.0 / 32.0
    wq = A["wq_dev"]
    wk = A["wk_dev"]
    wv = A["wv_dev"]
    wo = A["wo_dev"]
    bq = A["bq_row"]
    bk = A["bk_row"]
    bv = A["bv_cols"]
    bo = A["bo_row"]
    ident = A["ident"]
    scr = A["ascr"]
    xlnT = p.sb("xlnT", [128, NT * D], BF16)
    xln = p.sb("xln0", [128, D], BF16)
    qall = p.sb("qall", [128, D], BF16)
    kdup = p.sb("kdup", [128, 2 * D], BF16)
    vt = p.sb("vt", [128, D], BF16)
    psb_ = p.sb("psb", [128, 2048], BF16)
    ptsb = p.sb("ptsb", [128, 2048], BF16)
    pvall = p.sb("pvall", [128, D], BF16)
    a_tm = p.sb("a_tm", [128, D], BF16)
    a_fm = p.sb("a_fm", [128, D], BF16)
    negMall = p.sb("negMall", [128, 16])
    rsum = p.sb("rsum", [128, 16])
    rsinv = p.sb("rsinv", [128, 16])
    ones_bf = p.sb("ones_bf", [1, 128], BF16)
    wq_sb = p.sb("wq_sb", [128, 8 * D], BF16)
    wk_sb = p.sb("wk_sb", [128, 8 * D], BF16)
    wv_sb = p.sb("wv_sb", [128, 8 * D], BF16)
    wo_sb = p.sb("wo_sb", [128, 8 * D], BF16)
    bq_sb = p.sb("bq_sb", [1, D], BF16)
    bk_sb = p.sb("bk_sb", [1, D], BF16)
    bv_sb = p.sb("bv_sb", [128, 8])
    bo_sb = p.sb("bo_sb", [1, D], BF16)
    id_bf = p.sb("id_bf", [128, 128], BF16)

    psA = p.ps("psA", [128, 2048])
    psB = p.ps("psB", [128, D])
    psC = p.ps("psC", [128, D], BF16)
    ptr0 = psC

    psem = p.sem("psem", dma=True)
    s_id = p.sem("sid", dma=True)
    s_n0 = p.sem("n0")
    s_tp0 = p.sem("tp0")
    s_ev0 = p.sem("ev0")
    sev = [p.sb("sev0", [128, 2048]), p.sb("sev1", [128, 2048])]
    s_sev = p.sem("sev")
    s_qmm = p.sem("qmm")
    s_qev = p.sem("qev")
    s_kmm = p.sem("kmm")
    s_kev = p.sem("kev")
    s_vmm = p.sem("vmm")
    s_vev = p.sem("vev")
    s_smm = p.sem("smm")
    s_snm = p.sem("snm")
    s_pex = p.sem("pex")
    s_ptp = p.sem("ptp")
    s_ptev = p.sem("ptev")
    s_pvmm = p.sem("pvmm")
    s_pvev = p.sem("pvev")
    s_rr1 = [p.sem("rr1a", dma=True), p.sem("rr1b", dma=True)]
    s_rr2 = p.sem("rr2", dma=True)
    s_atp = p.sem("atp")
    s_afev = p.sem("afev")
    s_omm2 = p.sem("omm2")
    s_ares = p.sem("ares")

    @p.on("sync")
    def _(sy):
        for c in range(NC):
            WG(sy, s_pvev, 16 * (c + 1))
            if c >= 2:
                WG(sy, s_rr2, 16 * (c - 1))
            sy.dma_start(out=scr[c % 2], in_=pvall[:, :]).then_inc(s_rr1[c % 2].h, 16)
            WG(sy, s_rr1[c % 2], 16 * (c // 2 + 1))
            WG(sy, s_atp, c)
            src = bass.AP(scr[c % 2].tensor, 0,
                          [[512, 2], [64, 8], [16384, 8], [1024, 16], [1, 64]])
            sy.dma_start(out=a_tm[:, :], in_=src).then_inc(s_rr2.h, 16)

    @p.on("gpsimd")
    def _(g):
        g.dma_start(out=id_bf[:, :], in_=ident).then_inc(s_id.h, 16)
        g.dma_start(out=wq_sb[:, :], in_=wq).then_inc(psem.h, 16)
        g.dma_start(out=wk_sb[:, :], in_=wk).then_inc(psem.h, 16)
        g.dma_start(out=wv_sb[:, :], in_=wv).then_inc(psem.h, 16)
        g.dma_start(out=wo_sb[:, :], in_=wo).then_inc(psem.h, 16)
        g.dma_start(out=bq_sb[:, :], in_=bq).then_inc(psem.h, 16)
        g.dma_start(out=bk_sb[:, :], in_=bk).then_inc(psem.h, 16)
        g.dma_start(out=bv_sb[:, :], in_=bv).then_inc(psem.h, 16)
        g.dma_start(out=bo_sb[:, :], in_=bo).then_inc(psem.h, 16)

    rstd, negm, s_rstd = emit_ln(p, sx, gate, "ln", sq_buf=sev[0][:, 0:1024])

    @p.on("vector")
    def _(v):
        p.vop(v, "memset", ones_bf[:, :], 1.0)
        WG(v, s_rstd, 1)
        for t in range(NT):
            if t >= 1:
                WG(v, s_tp0, t)
            p.vop(v, "tensor_scalar", xln[:, :], sx[:, t * D:(t + 1) * D],
                  negm[:, t:t + 1], rstd[:, t:t + 1], ALU.add, ALU.mult, inc=s_n0)
            WG(v, s_tp0, t + 1)
            p.vop(v, "tensor_copy", xlnT[:, t * D:(t + 1) * D], ptr0[:, :], inc=s_ev0)
        for c in range(NC):
            if c >= 1:
                WG(v, s_rr1[(c - 1) % 2], 16 * ((c - 1) // 2 + 1))
            WG(v, s_vmm, c + 1)
            if c >= 1:
                WG(v, s_pvmm, 16 * c)
            for fc in range(8):
                p.vop(v, "tensor_scalar", vt[:, fc * 128:(fc + 1) * 128],
                      psB[:, fc * 128:(fc + 1) * 128], bv_sb[:, fc:fc + 1], None,
                      ALU.add, inc=(s_vev if fc == 7 else None))
            for rb in range(16):
                K = 16 * c + rb
                WG(v, s_smm, K + 1)
                p.vop(v, "tensor_reduce", negMall[:, rb:rb + 1], psA[:, :], AX,
                      ALU.max, negate=True, inc=s_snm)
                for hh in range(2):
                    WG(v, s_ptp, 2 * K + hh + 1)
                    for q2 in range(2):
                        p.vop(v, "tensor_copy",
                              ptsb[:, hh * 1024 + q2 * 512:hh * 1024 + (q2 + 1) * 512],
                              psC[:, q2 * 512:(q2 + 1) * 512],
                              inc=(s_ptev if q2 == 1 else None))
                WG(v, s_pex, K + 1)
                p.vop(v, "reciprocal", rsinv[:, rb:rb + 1], rsum[:, rb:rb + 1])
                WG(v, s_pvmm, K + 1)
                p.vop(v, "tensor_scalar", pvall[:, rb * 64:(rb + 1) * 64],
                      psB[:, 0:64], rsinv[:, rb:rb + 1], None, ALU.mult, inc=s_pvev)
            WG(v, s_atp, c + 1)
            p.vop(v, "tensor_copy", a_fm[:, :], ptr0[:, :], inc=s_afev)
            WG(v, s_omm2, c + 1)
            for h in range(2):
                p.vop(v, "scalar_tensor_tensor",
                      sx[:, c * D + h * 512:c * D + (h + 1) * 512],
                      psB[:, h * 512:(h + 1) * 512], 1.0,
                      sx[:, c * D + h * 512:c * D + (h + 1) * 512],
                      ALU.mult, ALU.add, inc=s_ares)

    @p.on("scalar")
    def _(s):
        for c in range(NC):
            WG(s, s_qmm, c + 1)
            if c >= 1:
                WG(s, s_smm, 16 * c)
            p.sop(s, "activation", qall[0:64, :], psB[0:64, :], AF.Copy, scale=SCALE)
            p.sop(s, "activation", qall[64:128, :], psB[64:128, :], AF.Copy,
                  scale=SCALE, inc=s_qev)
            WG(s, s_kmm, c + 1)
            p.sop(s, "activation", kdup[:, :], psA[:, :], AF.Copy, scale=SCALE,
                  inc=s_kev)
            def _exp(s, c, j):
                J = 16 * c + j
                WG(s, s_snm, J + 1)
                WG(s, s_ptp, 2 * J)
                p.sop(s, "activation", psb_[:, :], sev[J % 2][:, :], AF.Exp,
                      bias=negMall[:, j:j + 1], accum_out=rsum[:, j:j + 1],
                      inc=s_pex)
            for rb in range(16):
                K = 16 * c + rb
                WG(s, s_smm, K + 1)
                if K >= 2:
                    WG(s, s_snm, K - 1)
                p.sop(s, "activation", sev[K % 2][:, :], psA[:, :], AF.Copy,
                      scale=1.0, inc=s_sev)
                if rb >= 1:
                    _exp(s, c, rb - 1)
            _exp(s, c, 15)

    @p.on("tensor")
    def _(te):
        WG(te, s_id, 16)
        for t in range(NT):
            WG(te, s_n0, t + 1)
            if t >= 1:
                WG(te, s_ev0, t)
            for fc in range(8):
                ins = te.transpose(ptr0[:, fc * 128:(fc + 1) * 128],
                                   xln[:, fc * 128:(fc + 1) * 128], id_bf[:, :])
            ins.then_inc(s_tp0.h, 1)
        WG(te, s_ev0, NT)
        WG(te, psem, 16 * 8)
        for c in range(NC):
            cb = c * 8 * 128
            WG(te, s_ares, 2 * c)
            for par in range(2):
                for fh in range(2):
                    for kc in range(8):
                        ins = te.matmul(
                            psB[par * 64:(par + 1) * 64, fh * 512:(fh + 1) * 512],
                            xlnT[:, cb + kc * 128 + par: cb + (kc + 1) * 128: 2],
                            wq_sb[:, kc * D + fh * 512:kc * D + (fh + 1) * 512],
                            start=(kc == 0), stop=(kc == 7),
                            tile_position=(0, par * 64))
            for fh in range(2):
                ins = te.matmul(psB[:, fh * 512:(fh + 1) * 512], ones_bf[:, :],
                                bq_sb[:, fh * 512:(fh + 1) * 512],
                                start=False, stop=True, skip_group_check=True)
            ins.then_inc(s_qmm.h, 1)
            if c >= 1:
                WG(te, s_sev, 16 * c)
                WG(te, s_snm, 16 * c)
            for par in range(2):
                for half in range(2):
                    for fh in range(2):
                        for kc in range(8):
                            ins = te.matmul(
                                psA[half * 64:(half + 1) * 64,
                                    par * D + fh * 512:par * D + (fh + 1) * 512],
                                xlnT[:, cb + kc * 128 + par: cb + (kc + 1) * 128: 2],
                                wk_sb[:, kc * D + fh * 512:kc * D + (fh + 1) * 512],
                                start=(kc == 0), stop=(kc == 7),
                                tile_position=(0, half * 64))
            for par in range(2):
                for fh in range(2):
                    ins = te.matmul(psA[:, par * D + fh * 512:par * D + (fh + 1) * 512],
                                    ones_bf[:, :], bk_sb[:, fh * 512:(fh + 1) * 512],
                                    start=False, stop=True, skip_group_check=True)
            ins.then_inc(s_kmm.h, 1)
            WG(te, s_qev, c + 1)
            for fc in range(8):
                for kc in range(8):
                    ins = te.matmul(psB[:, fc * 128:(fc + 1) * 128],
                                    wv_sb[:, kc * D + fc * 128:kc * D + (fc + 1) * 128],
                                    xlnT[:, cb + kc * 128:cb + (kc + 1) * 128],
                                    start=(kc == 0), stop=(kc == 7))
            ins.then_inc(s_vmm.h, 1)
            WG(te, s_kev, c + 1)
            if c >= 1:
                WG(te, s_afev, c)
            def _ptpv(te, c, j):
                J = 16 * c + j
                WG(te, s_pex, J + 1)
                for hh in range(2):
                    WG(te, s_ptev, 2 * J + hh)
                    for j8 in range(8):
                        jj = hh * 8 + j8
                        ins = te.transpose(psC[:, j8 * 128:(j8 + 1) * 128],
                                           psb_[:, jj * 128:(jj + 1) * 128], id_bf[:, :])
                    ins.then_inc(s_ptp.h, 1)
                WG(te, s_ptev, 2 * (J + 1))
                WG(te, s_vev, c + 1)
                if J >= 1:
                    WG(te, s_pvev, J)
                for k2c in range(16):
                    p2, f2c = k2c // 8, k2c % 8
                    ins = te.matmul(psB[:, 0:64],
                                    ptsb[:, k2c * 128:(k2c + 1) * 128],
                                    vt[:, f2c * 128 + p2:f2c * 128 + 128:2],
                                    start=(k2c == 0), stop=(k2c == 15))
                ins.then_inc(s_pvmm.h, 1)
            for rb in range(16):
                K = 16 * c + rb
                p1, f1c = rb // 8, rb % 8
                WG(te, s_sev, K)
                WG(te, s_snm, K)
                for p2 in range(2):
                    for fh in range(2):
                        ins = te.matmul(
                            psA[:, p2 * D + fh * 512:p2 * D + (fh + 1) * 512],
                            qall[p1 * 64:(p1 + 1) * 64, f1c * 128:(f1c + 1) * 128],
                            kdup[p1 * 64:(p1 + 1) * 64, p2 * D + fh * 512:p2 * D + (fh + 1) * 512],
                            start=True, stop=True)
                ins.then_inc(s_smm.h, 1)
                if rb >= 1:
                    _ptpv(te, c, rb - 1)
            _ptpv(te, c, 15)
            WG(te, s_rr2, 16 * (c + 1))
            WG(te, s_ptev, 32 * (c + 1))
            for fc in range(8):
                ins = te.transpose(ptr0[:, fc * 128:(fc + 1) * 128],
                                   a_tm[:, fc * 128:(fc + 1) * 128], id_bf[:, :])
            ins.then_inc(s_atp.h, 1)
            WG(te, s_afev, c + 1)
            WG(te, s_pvev, 16 * (c + 1))
            for h in range(2):
                for dc in range(8):
                    ins = te.matmul(psB[:, h * 512:(h + 1) * 512],
                                    a_fm[:, dc * 128:(dc + 1) * 128],
                                    wo_sb[:, dc * D + h * 512:dc * D + (h + 1) * 512],
                                    start=(dc == 0), stop=False)
                ins2 = te.matmul(psB[:, h * 512:(h + 1) * 512], ones_bf[:, :],
                                 bo_sb[:, h * 512:(h + 1) * 512],
                                 start=False, stop=True)
            ins2.then_inc(s_omm2.h, 1)
    s_tp0.v += NT


    s_id.v = 16
    psem.v = 128
    s_rr1[0].v = 128
    s_rr1[1].v = 128
    s_rr2.v = 256
    s_qmm.v = 16
    s_kmm.v = 16
    s_vmm.v = 16
    s_smm.v = 256
    s_ptp.v = 512
    s_pvmm.v = 256
    s_atp.v = 16
    s_omm2.v = 16
    return s_ares, 2 * NC


def attn_aps(nc):
    return {
        "wq_dev": nc.dram_tensor("wq_dev", [128, 8 * D], F32, kind="ExternalInput").ap(),
        "wk_dev": nc.dram_tensor("wk_dev", [128, 8 * D], F32, kind="ExternalInput").ap(),
        "wv_dev": nc.dram_tensor("wv_dev", [128, 8 * D], F32, kind="ExternalInput").ap(),
        "wo_dev": nc.dram_tensor("wo_dev", [128, 8 * D], F32, kind="ExternalInput").ap(),
        "bq_row": nc.dram_tensor("bq_row", [1, D], F32, kind="ExternalInput").ap(),
        "bk_row": nc.dram_tensor("bk_row", [1, D], F32, kind="ExternalInput").ap(),
        "bv_cols": nc.dram_tensor("bv_cols", [128, 8], F32, kind="ExternalInput").ap(),
        "bo_row": nc.dram_tensor("bo_row", [1, D], F32, kind="ExternalInput").ap(),
        "ascr": [nc.dram_tensor(f"ascr{i}", [128, D], BF16).ap() for i in range(2)],
    }


def build_attn(dbg=False):
    return _standalone(emit_attn, attn_aps)


def _stage_attn(inputs):
    c = np.ascontiguousarray
    def wdev(w):
        return c(w.reshape(8, 128, D).transpose(1, 0, 2).reshape(128, 8 * D))
    return {
        "wq_dev": wdev(inputs["L1_attn_Wq"]),
        "wk_dev": wdev(inputs["L1_attn_Wk"]),
        "wv_dev": wdev(inputs["L1_attn_Wv"]),
        "wo_dev": wdev(inputs["L1_attn_Wo"]),
        "bq_row": c(inputs["L1_attn_bq"].reshape(1, D)),
        "bk_row": c(inputs["L1_attn_bk"].reshape(1, D)),
        "bv_cols": c(inputs["L1_attn_bv"].reshape(8, 128).T.reshape(128, 8)),
        "bo_row": c(inputs["L1_attn_bo"].reshape(1, D)),
        "ident": np.eye(128, dtype=np.float32),
    }



MAGIC = 12582912.0


def emit_bitffn(p, A, sx, gate, wprep=True):
    NSL1, NSL2 = 8, 32
    NW1 = float(4096 * 1024)
    w1t = A["w1t_dev"]
    w2t = A["w2t_dev"]
    cw = A["cw_dev"]
    cb = A["cb_dev"]
    ident = A["ident"]
    w1q_scr = A["w1q_scr"]
    w2q_scr = A["w2q_scr"]
    dqscr = A["dqscr"]
    s2scr = A["s2scr"]
    mc_scr = A["mc_scr"]
    xqT = p.sb("xqT", [128, 8 * T], BF16)
    xln = p.sb("xln0", [128, D], BF16)
    f32t = p.sb("f32t", [128, D])
    hsil = p.sb("hsil", [128, 32 * 512], BF16)
    hq = hsil
    w1s = [p.sb(f"w1s{i}", [128, 1024], BF16) for i in range(2)]
    w2s = [p.sb(f"w2s{i}", [128, 4096], BF16) for i in range(2)]
    cvsrc = p.sb("cvsrc", [128, 514])
    cvb = p.sb("cvb", [128, 512])
    a1c = p.sb("a1c", [128, 8])
    a2c = p.sb("a2c", [128, 32])
    r12 = p.sb("r12", [128, 2])
    mc1 = p.sb("mc1", [128, 1])
    mc2 = p.sb("mc2", [128, 1])
    ws1c = p.sb("ws1c", [128, 1])
    ws2c = p.sb("ws2c", [128, 1])
    amaxall = p.sb("amaxall", [128, NT])
    amcall = p.sb("amcall", [128, NT])
    sxc = p.sb("sxc", [128, NT])
    dq1all = p.sb("dq1all", [128, NT])
    dq1cols = p.sb("dq1cols", [16, 128])
    dq1row = p.sb("dq1row", [1, T])
    amax2 = p.sb("amax2", [128, 4])
    rtmp4 = p.sb("rtmp4", [128, 4])
    amax2c = p.sb("amax2c", [128, 4])
    s2col = p.sb("s2col", [128, 4])
    dqh2 = p.sb("dqh2", [128, 4])
    s2cols4 = p.sb("s2cols4", [4, 128])
    s2row = p.sb("s2row", [1, 512])
    cw_sb = p.sb("cw_sb", [128, 96])
    cb_sb = p.sb("cb_sb", [128, 32])
    ones_bf = p.sb("ones_bf", [1, 128], BF16)
    onesf = p.sb("onesf", [128, 128])
    onesf1 = p.sb("onesf1", [1, 128])
    id_bf = p.sb("id_bf", [128, 128], BF16)
    id_f = p.sb("id_f", [128, 128])

    psum_h = p.ps("psum_h", [128, 514])
    psum_s = p.ps("psum_s", [128, 514])
    psum_w2 = p.ps("psum_w2", [128, 512])
    ptr0 = p.ps("ptr0", [128, 512], BF16)
    psum_c = p.ps("psum_c", [128, 128])

    psem = p.sem("psem", dma=True)
    s_id = p.sem("sid", dma=True)
    s_sl = [p.sem("sl0", dma=True), p.sem("sl1", dma=True)]
    s_abs = p.sem("abs")
    s_tot = p.sem("tot")
    s_wsc = p.sem("wsc")
    s_q3 = p.sem("q3")
    s_qst = [p.sem("qst0", dma=True), p.sem("qst1", dma=True)]
    s_n0 = p.sem("n0")
    s_amx = p.sem("amx")
    s_sxc = p.sem("sxc1")
    s_xq = p.sem("xq")
    s_tp0 = p.sem("tp0")
    s_ev0 = p.sem("ev0")
    s_dqt = p.sem("dqt")
    s_dqcv = p.sem("dqcv")
    s_dqb = p.sem("dqb", dma=True)
    s_w1l = [p.sem("w1l0", dma=True), p.sem("w1l1", dma=True)]
    s_w2l = [p.sem("w2l0", dma=True), p.sem("w2l1", dma=True)]
    s_dqr = p.sem("dqr")
    s_dqe = p.sem("dqe")
    s_hmm = p.sem("hmm")
    s_cv = p.sem("cv")
    s_sil = p.sem("sil")
    s_htp = p.sem("htp")
    s_hmx = p.sem("hmx")
    s_s2c = p.sem("s2c")
    s_s2t = p.sem("s2t")
    s_s2cv = p.sem("s2cv")
    s_s2b = p.sem("s2b", dma=True)
    s_s2r = p.sem("s2r")
    s_s2e = p.sem("s2e")
    s_hq = p.sem("hq3")
    s_wmm = p.sem("wmm")
    s_oev = p.sem("oev")
    s_otp = p.sem("otp")
    s_ores = p.sem("ores")
    s_mcl = p.sem("mcl", dma=True)
    s_mcv = p.sem("mcv")

    rstd, negm, s_rstd = emit_ln(p, sx, gate, "ln")

    _w0, _w0cm = p.sb2("wsl0", [128, 4096])
    wslab = [_w0, _w0]
    qslab = [hsil[:, 0:4096], hsil[:, 4096:8192]]
    _w0cm.__exit__(None, None, None)
    dqrep = p.sb("dqrep", [128, 514])
    s2rep = p.sb("s2rep", [128, 512])
    otmp = p.sb("otmp", [128, 512], BF16)

    @p.on("sync")
    def _(sy):
        for i in range(NSL1 + NSL2 if wprep else 0):
            if i >= 1:
                WG(sy, s_abs, i)
            if i < NSL1:
                sy.dma_start(out=wslab[i % 2][:, :],
                             in_=w1t[i * 128:(i + 1) * 128, :]).then_inc(s_sl[i % 2].h, 16)
            else:
                j = i - NSL1
                sy.dma_start(out=wslab[i % 2][:, 0:1024],
                             in_=w2t[j * 128:(j + 1) * 128, :]).then_inc(s_sl[i % 2].h, 16)
        for i in range(NSL1 + NSL2 if wprep else 0):
            ii = NSL1 + NSL2 + i
            if i == 0:
                WG(sy, s_abs, 40)
            else:
                WG(sy, s_q3, i)
            if i < NSL1:
                sy.dma_start(out=wslab[ii % 2][:, :],
                             in_=w1t[i * 128:(i + 1) * 128, :]).then_inc(s_sl[ii % 2].h, 16)
            else:
                j = i - NSL1
                sy.dma_start(out=wslab[ii % 2][:, 0:1024],
                             in_=w2t[j * 128:(j + 1) * 128, :]).then_inc(s_sl[ii % 2].h, 16)
            WG(sy, s_q3, i + 1)
            if i < NSL1:
                dst = bass.AP(w1q_scr.tensor, i * 128,
                              [[1024, 128], [128 * 1024, 32], [1, 128]])
                sy.dma_start(out=dst, in_=qslab[i % 2][:, :]).then_inc(s_qst[i % 2].h, 16)
            else:
                j = i - NSL1
                dst = bass.AP(w2q_scr.tensor, j * 128,
                              [[4096, 128], [128 * 4096, 8], [1, 128]])
                sy.dma_start(out=dst, in_=qslab[i % 2][0:128, 0:1024]).then_inc(s_qst[i % 2].h, 16)
        WG(sy, s_dqcv, 1)
        sy.dma_start(out=dqscr, in_=dq1cols[:, :]).then_inc(s_dqb.h, 16)
        WG(sy, s_dqb, 16)
        sy.dma_start(out=dq1row[:, :],
                     in_=dqscr.rearrange("(one a) b -> one (a b)", one=1)).then_inc(s_dqb.h, 16)
        for tc in range(4):
            for fo in range(32):
                i = tc * 32 + fo
                if i >= 2:
                    WG(sy, s_hmm, i - 1)
                sy.dma_start(out=w1s[i % 2][:, :],
                             in_=w1q_scr[fo, :, :]).then_inc(s_w1l[i % 2].h, 16)
            WG(sy, s_s2cv, tc + 1)
            sy.dma_start(out=s2scr[tc, :, :], in_=s2cols4[:, :]).then_inc(s_s2b.h, 16)
            WG(sy, s_s2b, 32 * tc + 16)
            if tc >= 1:
                WG(sy, s_s2r, tc)
            sy.dma_start(out=s2row[:, :],
                         in_=s2scr[tc].rearrange("(one a) b -> one (a b)", one=1)).then_inc(s_s2b.h, 16)
            for dc in range(8):
                i = tc * 8 + dc
                if i >= 2:
                    WG(sy, s_wmm, i - 1)
                sy.dma_start(out=w2s[i % 2][:, :],
                             in_=w2q_scr[dc, :, :]).then_inc(s_w2l[i % 2].h, 16)
        if wprep:
            WG(sy, s_mcv, 1)
            sy.dma_start(out=mc_scr[0, :], in_=mc1[:, :]).then_inc(s_mcl.h, 16)
            sy.dma_start(out=mc_scr[1, :], in_=mc2[:, :]).then_inc(s_mcl.h, 16)

    @p.on("gpsimd")
    def _(g):
        g.dma_start(out=id_bf[:, :], in_=ident).then_inc(s_id.h, 16)
        g.dma_start(out=cw_sb[:, :], in_=cw).then_inc(psem.h, 16)
        g.dma_start(out=cb_sb[:, :], in_=cb).then_inc(psem.h, 16)
        g.dma_start(out=id_f[:, :], in_=ident).then_inc(psem.h, 16)
        if not wprep:
            g.dma_start(out=mc1[:, :], in_=mc_scr[0, :]).then_inc(s_mcl.h, 16)
            g.dma_start(out=mc2[:, :], in_=mc_scr[1, :]).then_inc(s_mcl.h, 16)

    @p.on("scalar")
    def _(s):
        for i in range(NSL1 + NSL2 if wprep else 0):
            WG(s, s_sl[i % 2], 16 * (i // 2 + 1))
            if i < NSL1:
                p.sop(s, "activation", qslab[i % 2][:, :], wslab[i % 2][:, :],
                      AF.Abs, accum_out=a1c[:, i:i + 1], inc=s_abs)
            else:
                j = i - NSL1
                p.sop(s, "activation", qslab[i % 2][0:128, 0:1024], wslab[i % 2][:, 0:1024],
                      AF.Abs, accum_out=a2c[:, j:j + 1], inc=s_abs)
        WG(s, psem, 48)
        for tc in range(4):
            for fo in range(32):
                i = tc * 32 + fo
                WG(s, s_cv, i + 1)
                if wprep and tc == 0 and fo == 0:
                    WG(s, s_qst[0], 16 * 20)
                    WG(s, s_qst[1], 16 * 20)
                if tc >= 1 and fo == 0:
                    WG(s, s_wmm, 8 * tc)
                p.sop(s, "activation", hsil[:, fo * 512:(fo + 1) * 512], cvb[:, :],
                      AF.Silu, bias=cb_sb[:, fo:fo + 1], inc=s_sil)
            for dc in range(8):
                i = tc * 8 + dc
                WG(s, s_wmm, i + 1)
                if i >= 1:
                    WG(s, s_otp, i)
                p.sop(s, "activation", otmp[:, :], psum_w2[:, :], AF.Copy, inc=s_oev)

    @p.on("tensor")
    def _(te):
        WG(te, s_id, 16)
        WG(te, psem, 48)
        if wprep:
            WG(te, s_wsc, 1)
            te.matmul(psum_c[:, 0:1], onesf[:, :], r12[:, 0:1], start=True, stop=True)
            te.matmul(psum_c[:, 1:2], onesf[:, :], r12[:, 1:2],
                      start=True, stop=True).then_inc(s_tot.h, 1)
        for t in range(NT):
            WG(te, s_xq, t + 1)
            for hh in range(2):
                if 2 * t + hh >= 1:
                    WG(te, s_ev0, 2 * t + hh)
                for fc4 in range(4):
                    fc = hh * 4 + fc4
                    ins = te.transpose(ptr0[:, fc4 * 128:(fc4 + 1) * 128],
                                       xln[:, fc * 128:(fc + 1) * 128], id_bf[:, :])
                ins.then_inc(s_tp0.h, 1)
        WG(te, s_sxc, 1)
        te.matmul(psum_c[0:16, 0:128], dq1all[:, :], id_f[:, :],
                  start=True, stop=True).then_inc(s_dqt.h, 1)
        WG(te, s_ev0, 2 * NT)
        WG(te, s_dqb, 32)
        for tc in range(4):
            lo = tc * 512 - 1
            lo0 = max(lo, 0)
            hi0 = min(lo + 513, T - 1)
            nv = hi0 - lo0 + 1
            WG(te, s_s2e, tc)
            ins = te.matmul(psum_s[:, 0:512], onesf1[:, :],
                            dq1row[:, lo0:lo0 + 512], start=True, stop=True)
            ins = te.matmul(psum_s[:, 512:nv], onesf1[:, :],
                            dq1row[:, lo0 + 512:lo0 + nv], start=True, stop=True)
            ins.then_inc(s_dqr.h, 1)
            for fo in range(32):
                i = tc * 32 + fo
                WG(te, s_w1l[i % 2], 16 * (i // 2 + 1))
                if i >= 1:
                    WG(te, s_cv, i)
                for kc in range(8):
                    ins = te.matmul(psum_h[:, 0:512],
                                    w1s[i % 2][:, kc * 128:(kc + 1) * 128],
                                    xqT[:, kc * T + lo0:kc * T + lo0 + 512],
                                    start=(kc == 0), stop=(kc == 7))
                for kc in range(8):
                    ins = te.matmul(psum_h[:, 512:nv],
                                    w1s[i % 2][:, kc * 128:(kc + 1) * 128],
                                    xqT[:, kc * T + lo0 + 512:kc * T + lo0 + nv],
                                    start=(kc == 0), stop=(kc == 7))
                ins.then_inc(s_hmm.h, 1)
                WG(te, s_sil, i + 1)
                if i >= 1:
                    WG(te, s_hmx, i)
                if fo == 0:
                    WG(te, s_ores, 32 * tc)
                for ts4 in range(4):
                    ins = te.transpose(ptr0[:, ts4 * 128:(ts4 + 1) * 128],
                                       hsil[:, fo * 512 + ts4 * 128:fo * 512 + (ts4 + 1) * 128],
                                       id_bf[:, :])
                ins.then_inc(s_htp.h, 1)
            WG(te, s_s2c, tc + 1)
            if tc >= 1:
                WG(te, s_s2cv, tc)
            te.matmul(psum_c[0:4, 0:128], s2col[:, :], id_f[:, :],
                      start=True, stop=True).then_inc(s_s2t.h, 1)
            WG(te, s_s2b, 32 * tc + 32)
            WG(te, s_dqe, tc + 1)
            te.matmul(psum_s[:, 0:512], onesf1[:, :], s2row[:, :],
                      start=True, stop=True).then_inc(s_s2r.h, 1)
            for dc in range(8):
                i = tc * 8 + dc
                WG(te, s_w2l[i % 2], 16 * (i // 2 + 1))
                WG(te, s_hq, 32 * (tc + 1))
                if i >= 1:
                    WG(te, s_oev, i)
                for fi in range(32):
                    ins = te.matmul(psum_w2[:, :],
                                    w2s[i % 2][:, fi * 128:(fi + 1) * 128],
                                    hq[:, fi * 512:(fi + 1) * 512],
                                    start=(fi == 0), stop=(fi == 31))
                ins.then_inc(s_wmm.h, 1)
                WG(te, s_oev, i + 1)
                if i >= 1:
                    WG(te, s_ores, 4 * i)
                if dc == 0:
                    WG(te, s_hmx, 32 * (tc + 1))
                for ts4 in range(4):
                    ins = te.transpose(ptr0[:, ts4 * 128:(ts4 + 1) * 128],
                                       otmp[:, ts4 * 128:(ts4 + 1) * 128], id_bf[:, :])
                ins.then_inc(s_otp.h, 1)
    s_tp0.v += 2 * NT

    @p.on("vector")
    def _(v):
        p.vop(v, "memset", ones_bf[:, :], 1.0)
        p.vop(v, "memset", onesf[:, :], 1.0)
        p.vop(v, "memset", onesf1[:, :], 1.0)
        if wprep:
            WG(v, s_abs, 40)
            p.vop(v, "reduce_sum", r12[:, 0:1], a1c[:, :], AX)
            p.vop(v, "reduce_sum", r12[:, 1:2], a2c[:, :], AX, inc=s_wsc)
            WG(v, s_tot, 1)
            p.vop(v, "tensor_scalar", mc1[:, :], psum_c[:, 0:1], 1.0 / NW1, 1e-5,
                  ALU.mult, ALU.max)
            p.vop(v, "tensor_scalar", mc2[:, :], psum_c[:, 1:2], 1.0 / NW1, 1e-5,
                  ALU.mult, ALU.max, inc=s_mcv)
            p.vop(v, "reciprocal", ws1c[:, :], mc1[:, :])
            p.vop(v, "reciprocal", ws2c[:, :], mc2[:, :])
        else:
            WG(v, s_mcl, 32)
        for i in range(NSL1 + NSL2 if wprep else 0):
            ii = NSL1 + NSL2 + i
            WG(v, s_sl[ii % 2], 16 * (ii // 2 + 1))
            if i >= 2:
                WG(v, s_qst[i % 2], 16 * (i // 2))
            wcol = ws1c if i < NSL1 else ws2c
            ncols = 4096 if i < NSL1 else 1024
            p.vop(v, "tensor_scalar", wslab[ii % 2][:, 0:ncols], wslab[ii % 2][:, 0:ncols],
                  wcol[:, 0:1], MAGIC, ALU.mult, ALU.add)
            p.vop(v, "tensor_scalar", wslab[ii % 2][:, 0:ncols], wslab[ii % 2][:, 0:ncols],
                  MAGIC, -1.0, ALU.subtract, ALU.max)
            p.vop(v, "tensor_scalar", qslab[i % 2][0:128, 0:ncols], wslab[ii % 2][:, 0:ncols],
                  1.0, None, ALU.min, inc=s_q3)
        WG(v, s_rstd, 1)
        for t in range(NT):
            p.vop(v, "tensor_scalar", f32t[:, :], sx[:, t * D:(t + 1) * D],
                  negm[:, t:t + 1], rstd[:, t:t + 1], ALU.add, ALU.mult)
            p.vop(v, "tensor_reduce", amaxall[:, t:t + 1], f32t[:, :], AX, ALU.max,
                  apply_absolute_value=True, inc=s_amx)
        p.vop(v, "tensor_scalar", amcall[:, :], amaxall[:, :], 1e-5, None, ALU.max)
        p.vop(v, "reciprocal", sxc[:, :], amcall[:, :])
        p.vop(v, "tensor_scalar", sxc[:, :], sxc[:, :], 127.0, None, ALU.mult)
        p.vop(v, "tensor_scalar", dq1all[:, :], amcall[:, :], mc1[:, 0:1],
              1.0 / 127.0, ALU.mult, ALU.mult, inc=s_sxc)
        for t in range(NT):
            if t >= 1:
                WG(v, s_tp0, 2 * t)
            p.vop(v, "tensor_scalar", f32t[:, :], sx[:, t * D:(t + 1) * D],
                  negm[:, t:t + 1], rstd[:, t:t + 1], ALU.add, ALU.mult)
            p.vop(v, "tensor_scalar", f32t[:, :], f32t[:, :], sxc[:, t:t + 1],
                  MAGIC, ALU.mult, ALU.add)
            p.vop(v, "tensor_scalar", f32t[:, :], f32t[:, :], MAGIC, -128.0,
                  ALU.subtract, ALU.max)
            p.vop(v, "tensor_scalar", xln[:, :], f32t[:, :], 127.0, None,
                  ALU.min, inc=s_xq)
            for hh in range(2):
                WG(v, s_tp0, 2 * t + hh + 1)
                for fc4 in range(4):
                    fc = hh * 4 + fc4
                    p.vop(v, "tensor_copy",
                          xqT[:, fc * T + t * 128:fc * T + (t + 1) * 128],
                          ptr0[:, fc4 * 128:(fc4 + 1) * 128],
                          inc=(s_ev0 if fc4 == 3 else None))
        WG(v, s_dqt, 1)
        p.vop(v, "tensor_copy", dq1cols[:, :], psum_c[0:16, 0:128], inc=s_dqcv)
        for tc in range(4):
            lo = tc * 512 - 1
            lo0 = max(lo, 0)
            nv = min(lo + 513, T - 1) - lo0 + 1
            WG(v, s_dqr, tc + 1)
            p.vop(v, "tensor_copy", dqrep[:, 0:nv], psum_s[:, 0:nv], inc=s_dqe)
            off = -1 if tc == 0 else 0
            for fo in range(32):
                i = tc * 32 + fo
                WG(v, s_hmm, i + 1)
                p.vop(v, "tensor_tensor", cvsrc[:, 0:nv], psum_h[:, 0:nv],
                      dqrep[:, 0:nv], ALU.mult)
                p.vop(v, "tensor_scalar", cvb[:, :], cvsrc[:, 1 + off:513 + off],
                      cw_sb[:, fo * 3 + 1:fo * 3 + 2], None, ALU.mult)
                for k in (0, 2):
                    clo = k + off
                    rlo = max(0, -clo)
                    rhi = min(512, nv - clo)
                    p.vop(v, "scalar_tensor_tensor", cvb[:, rlo:rhi],
                          cvsrc[:, rlo + clo:rhi + clo],
                          cw_sb[:, fo * 3 + k:fo * 3 + k + 1], cvb[:, rlo:rhi],
                          ALU.mult, ALU.add, inc=(s_cv if k == 2 else None))
                WG(v, s_htp, i + 1)
                p.vop(v, "tensor_reduce", rtmp4[:, :],
                      ptr0[:, :].rearrange("p (a b) -> p a b", a=4), AX, ALU.max,
                      apply_absolute_value=True)
                if fo == 0:
                    p.vop(v, "tensor_copy", amax2[:, :], rtmp4[:, :], inc=s_hmx)
                else:
                    p.vop(v, "tensor_tensor", amax2[:, :], amax2[:, :],
                          rtmp4[:, :], ALU.max, inc=s_hmx)
            p.vop(v, "tensor_scalar", amax2c[:, :], amax2[:, :], 1e-5, None, ALU.max)
            p.vop(v, "reciprocal", s2col[:, :], amax2c[:, :])
            p.vop(v, "tensor_scalar", s2col[:, :], s2col[:, :], 127.0, None, ALU.mult)
            p.vop(v, "tensor_scalar", dqh2[:, :], amax2c[:, :], mc2[:, 0:1],
                  0.5 / 127.0, ALU.mult, ALU.mult, inc=s_s2c)
            WG(v, s_s2t, tc + 1)
            p.vop(v, "tensor_copy", s2cols4[:, :], psum_c[0:4, 0:128], inc=s_s2cv)
            WG(v, s_s2r, tc + 1)
            p.vop(v, "tensor_copy", s2rep[:, :], psum_s[:, 0:512], inc=s_s2e)
            for fo in range(32):
                i = tc * 32 + fo
                if tc >= 1 and fo == 0:
                    WG(v, s_wmm, 8 * tc)
                p.vop(v, "tensor_tensor", f32t[:, 0:512],
                      hsil[:, fo * 512:(fo + 1) * 512], s2rep[:, :], ALU.mult)
                p.vop(v, "tensor_scalar", f32t[:, 0:512], f32t[:, 0:512],
                      MAGIC, MAGIC, ALU.add, ALU.subtract)
                p.vop(v, "tensor_scalar", hq[:, fo * 512:(fo + 1) * 512],
                      f32t[:, 0:512], -128.0, 127.0, ALU.max, ALU.min, inc=s_hq)
            for dc in range(8):
                i = tc * 8 + dc
                WG(v, s_otp, i + 1)
                for ts4 in range(4):
                    tt = tc * 4 + ts4
                    p.vop(v, "scalar_tensor_tensor",
                          sx[:, tt * D + dc * 128:tt * D + (dc + 1) * 128],
                          ptr0[:, ts4 * 128:(ts4 + 1) * 128], dqh2[:, ts4:ts4 + 1],
                          sx[:, tt * D + dc * 128:tt * D + (dc + 1) * 128],
                          ALU.mult, ALU.add, inc=s_ores)


    s_id.v = 16
    psem.v = 48
    s_mcl.v = 32
    if wprep:
        s_sl[0].v = 640
        s_sl[1].v = 640
        s_qst[0].v = 320
        s_qst[1].v = 320
    s_dqb.v = 32
    s_w1l[0].v = 1024
    s_w1l[1].v = 1024
    s_w2l[0].v = 256
    s_w2l[1].v = 256
    s_s2b.v = 128
    if wprep:
        s_tot.v = 1
    s_dqt.v = 1
    s_dqr.v = 4
    s_hmm.v = 128
    s_htp.v = 128
    s_s2t.v = 4
    s_s2r.v = 4
    s_wmm.v = 32
    s_otp.v = 32
    return s_ores, 4 * 8 * 4


def bitffn_aps(nc):
    return {
        "w1t_dev": nc.dram_tensor("w1t_dev", [1024, 4096], F32, kind="ExternalInput").ap(),
        "w2t_dev": nc.dram_tensor("w2t_dev", [4096, 1024], F32, kind="ExternalInput").ap(),
        "cw_dev": nc.dram_tensor("cw_dev", [128, 96], F32, kind="ExternalInput").ap(),
        "cb_dev": nc.dram_tensor("cb_dev", [128, 32], F32, kind="ExternalInput").ap(),
        "w1q_scr": nc.dram_tensor("w1q_scr", [32, 128, 1024], BF16).ap(),
        "mc_scr": nc.dram_tensor("mc_scr", [2, 128], F32).ap(),
        "w2q_scr": nc.dram_tensor("w2q_scr", [8, 128, 4096], BF16).ap(),
        "dqscr": nc.dram_tensor("dqscr", [16, 128], F32).ap(),
        "s2scr": nc.dram_tensor("s2scr", [4, 4, 128], F32).ap(),
    }


def build_bitffn(dbg=False):
    return _standalone(emit_bitffn, bitffn_aps)


def _stage_bitffn(inputs):
    c = np.ascontiguousarray
    return {
        "w1t_dev": c(inputs["L1_ffn_w1"].T),
        "w2t_dev": c(inputs["L1_ffn_w2"].T),
        "cw_dev": c(inputs["L1_ffn_conv_w"].reshape(32, 128, 3).transpose(1, 0, 2).reshape(128, 96)),
        "cb_dev": c(inputs["L1_ffn_conv_b"].reshape(32, 128).T.reshape(128, 32)),
        "ident": np.eye(128, dtype=np.float32),
    }


def build_fused(nblocks=6, pool=True, start=0):
    nc = bass.Bass("TRN2", target_bir_lowering=False)
    x = nc.dram_tensor("x", [T, D], F32, kind="ExternalInput").ap()
    xo = nc.dram_tensor("xo", [T, D], F32, kind="ExternalOutput").ap()
    A = {}
    A.update(moe_aps(nc))
    A.update(mamba_aps(nc))
    A.update(attn_aps(nc))
    A.update(bitffn_aps(nc))
    A["ident"] = nc.dram_tensor("ident", [128, 128], F32, kind="ExternalInput").ap()
    with contextlib.ExitStack() as es:
        p = Prog(nc, es)
        if pool:
            p.pool = [es.enter_context(nc.semaphore(f"sp{i}")) for i in range(56)]
            p.pool_base = [0] * 56
        bar0 = p.sem("bar0")
        bar1 = p.sem("bar1")
        gdsem = p.sem("gdsem")
        sx = p.sb("sx", [128, NT * D])

        @p.on("sync")
        def _(sy):
            for t in range(NT):
                sy.dma_start(out=sx[:, t * D:(t + 1) * D],
                             in_=x[t * 128:(t + 1) * 128, :]).then_inc(gdsem.h, 16)

        bitffn_nw = lambda p, A, sx, gate: emit_bitffn(p, A, sx, gate, wprep=False)
        seq = [("L0fa", emit_moe), ("L0mx", emit_mamba), ("L0fb", emit_moe),
               ("L1fa", emit_bitffn), ("L1ax", emit_attn), ("L1fb", bitffn_nw)][start:start + nblocks]
        for k, (pref, em) in enumerate(seq):
            p.begin_block(pref)
            gate = (gdsem, 16 * NT) if k == 0 else (bar1, k)
            em(p, A, sx, gate)
            sems = p.end_block()
            emit_barrier(p, k, sems, bar0, bar1)

        @p.on("sync")
        def _(sy):
            WG(sy, bar1, len(seq))
            for t in range(NT):
                sy.dma_start(out=xo[t * 128:(t + 1) * 128, :],
                             in_=sx[:, t * D:(t + 1) * D]).then_inc(gdsem.h, 16)
            WG(sy, gdsem, 16 * NT + 16 * NT)

        p.finalize()
    return nc


def kernel(**inputs):
    x = np.ascontiguousarray(inputs["L0_attn_hidden_states"])
    W = {}
    W.update(_stage_moe(inputs))
    W.update(_stage_mamba(inputs))
    W.update(_stage_attn(inputs))
    W.update(_stage_bitffn(inputs))
    nc = build_fused()
    in_maps = [dict(W, x=np.ascontiguousarray(x[b])) for b in range(8)]
    res = run_bass_kernel_spmd(nc, in_maps, core_ids=list(range(8)))
    return np.stack([np.asarray(res.results[b]["xo"]) for b in range(8)], 0)


# version 11 (final)
# speedup vs baseline: 1.1089x; 1.1089x over previous
import contextlib
import numpy as np
import concourse.bass as bass
import concourse.mybir as mybir
from concourse.alu_op_type import AluOpType as ALU
from concourse.bass_utils import run_bass_kernel_spmd

F32 = mybir.dt.float32
BF16 = mybir.dt.bfloat16
AF = mybir.ActivationFunctionType
AX = mybir.AxisListType.X

T = 2048
D = 1024
NT = T // 128
EPS = 1e-5


class Sem:
    __slots__ = ("h", "v", "_pb", "_slot")
    def __init__(self, h, pb=None, slot=None):
        self.h = h
        self.v = 0
        self._pb = pb
        self._slot = slot

    @property
    def b(self):
        return self._pb[self._slot] if self._pb is not None else 0


def WG(eng, sem, v):
    eng.wait_ge(sem.h, sem.b + v)


class Prog:
    ENGS = ("sync", "gpsimd", "vector", "scalar", "tensor")

    def __init__(self, nc, es):
        self.nc = nc
        self.es = es
        self.gq = []
        self.pool = None
        self.pool_base = []
        self._pc = 0
        self.pref = ""
        self.block_sems = None
        self.block_cms = None
        self.semmap = {}
        self.chains = {"vector": self.sem("chv"), "scalar": self.sem("chs"),
                       "gpsimd": self.sem("chg")}
        self.last = {"vector": None, "scalar": None, "gpsimd": None}

    def sem(self, name, dma=False):
        if self.pool is not None and self.block_sems is not None:
            if dma:
                self._pcd -= 1
                slot = self._pcd
            else:
                slot = self._pc
                self._pc += 1
            assert self._pc <= self._pcd, "sem pool exhausted"
            h = self.pool[slot]
            self.semmap.setdefault(self.pref, {})[f"sp{slot}"] = name
            sm = Sem(h, self.pool_base, slot)
            self.block_sems.append(sm)
            return sm
        sm = Sem(self.es.enter_context(self.nc.semaphore(self.pref + name)))
        if self.block_sems is not None:
            self.block_sems.append(sm)
        return sm

    def begin_block(self, pref):
        self.pref = pref
        self._pc = 0
        self._pcd = len(self.pool) if self.pool is not None else 0
        self.block_sems = []
        self.block_cms = []
        self.last = {"vector": None, "scalar": None, "gpsimd": None}

    def end_block(self):
        sems = self.block_sems
        for cm in reversed(self.block_cms):
            cm.__exit__(None, None, None)
        self.block_sems = None
        self.block_cms = None
        self.pref = ""
        return sems

    def on(self, engname):
        def deco(f):
            self.gq.append((engname, f))
            return None
        return deco

    def on_if(self, cond, engname):
        if cond:
            return self.on(engname)
        def deco(f):
            return None
        return deco

    def sb(self, name, shape, dt=F32):
        cm = self.nc.sbuf_tensor(self.pref + name, shape, dt)
        if self.block_cms is not None:
            h = cm.__enter__()
            self.block_cms.append(cm)
            return h
        return self.es.enter_context(cm)

    def ps(self, name, shape, dt=F32):
        cm = self.nc.psum_tensor(self.pref + name, shape, dt)
        if self.block_cms is not None:
            h = cm.__enter__()
            self.block_cms.append(cm)
            return h
        return self.es.enter_context(cm)

    def ps2(self, name, shape, dt=F32):
        cm = self.nc.psum_tensor(self.pref + name, shape, dt)
        return cm.__enter__(), cm

    def sb2(self, name, shape, dt=F32):
        cm = self.nc.sbuf_tensor(self.pref + name, shape, dt)
        return cm.__enter__(), cm

    def chain(self, engname, eng, fname, *a, inc=None, **kw):
        last = self.last[engname]
        if last is not None:
            eng.wait_ge(last[0], last[1])
        ins = getattr(eng, fname)(*a, **kw)
        s = inc if inc is not None else self.chains[engname]
        s.v += 1
        ins.then_inc(s.h, 1)
        self.last[engname] = (s.h, s.b + s.v)
        return ins

    def vop(self, v, fname, *a, inc=None, **kw):
        return self.chain("vector", v, fname, *a, inc=inc, **kw)

    def sop(self, s, fname, *a, inc=None, **kw):
        return self.chain("scalar", s, fname, *a, inc=inc, **kw)

    def gop(self, g, fname, *a, inc=None, **kw):
        return self.chain("gpsimd", g, fname, *a, inc=inc, **kw)

    def finalize(self):
        nc = self.nc
        gq = self.gq
        with nc.Block() as block:
            def runner(_eng, gq=gq, nc=nc):
                for engname, f in gq:
                    f(getattr(nc, engname))
            block.sync(runner)


def emit_barrier(p, k, blocksems, bar0, bar1):
    @p.on("sync")
    def _(sy):
        for s in blocksems:
            if s.v:
                WG(sy, s, s.v)
        p.nc.all_engine_barrier()
        sy.sem_inc(bar1.h, 1)

    for e in ("gpsimd", "vector", "scalar", "tensor"):
        @p.on(e)
        def _(eng):
            WG(eng, bar1, k + 1)

    @p.on("sync")
    def _(sy):
        for s in blocksems:
            if s._pb is not None:
                s._pb[s._slot] += s.v
        p.last["vector"] = None
        p.last["scalar"] = None
        p.last["gpsimd"] = None


def emit_ln(p, sx, gate, pref, n_tiles=NT, sq_buf=None):
    sq = sq_buf if sq_buf is not None else p.sb(pref + "sq", [128, D])
    s1 = p.sb(pref + "s1", [128, n_tiles])
    s2 = p.sb(pref + "s2", [128, n_tiles])
    mall = p.sb(pref + "m", [128, n_tiles])
    var = p.sb(pref + "var", [128, n_tiles])
    var2 = p.sb(pref + "var2", [128, n_tiles])
    msq = p.sb(pref + "msq", [128, n_tiles])
    lntmp = p.sb(pref + "lntmp", [128, n_tiles])
    rstd = p.sb(pref + "rstd", [128, n_tiles])
    negm = p.sb(pref + "negm", [128, n_tiles])
    s_stat = p.sem(pref + "stat")
    s_var = p.sem(pref + "var")
    s_rstd = p.sem(pref + "rstd")
    gh, gv = gate

    @p.on("scalar")
    def _(s):
        WG(s, gh, gv)
        for t in range(n_tiles):
            p.sop(s, "activation", sq[:, 0:D], sx[:, t * D:(t + 1) * D], AF.Square,
                  accum_out=s2[:, t:t + 1], inc=(s_stat if t == n_tiles - 1 else None))
        WG(s, s_var, 1)
        p.sop(s, "activation", lntmp[:, :], var2[:, :], AF.Ln)
        p.sop(s, "activation", rstd[:, :], lntmp[:, :], AF.Exp, scale=-0.5, inc=s_rstd)

    @p.on("vector")
    def _(v):
        WG(v, gh, gv)
        for t in range(n_tiles):
            p.vop(v, "reduce_sum", s1[:, t:t + 1], sx[:, t * D:(t + 1) * D], AX)
        WG(v, s_stat, 1)
        p.vop(v, "tensor_scalar", mall[:, :], s1[:, :], 1.0 / D, None, ALU.mult)
        p.vop(v, "tensor_scalar", var[:, :], s2[:, :], 1.0 / D, EPS, ALU.mult, ALU.add)
        p.vop(v, "tensor_tensor", msq[:, :], mall[:, :], mall[:, :], ALU.mult)
        p.vop(v, "tensor_tensor", var2[:, :], var[:, :], msq[:, :], ALU.subtract, inc=s_var)
        p.vop(v, "tensor_scalar", negm[:, :], mall[:, :], -1.0, None, ALU.mult)
    return rstd, negm, s_rstd



def emit_moe(p, A, sx, gate):
    rw = A["rw_dev"]
    rb = A["rb"]
    ew = A["ew_dev"]
    eb = A["eb_row"]
    ident = A["ident"]
    xlnT = p.sb("xlnT", [128, NT * D], BF16)
    xln = p.sb("xln0", [128, D], BF16)
    probs = p.sb("probs", [128, 8 * NT])
    sumall = p.sb("sumall", [128, NT])
    ghalf = p.sb("ghalf", [128, NT])
    ghalf2 = p.sb("ghalf2", [128, NT])
    negM = p.sb("negM", [128, NT])
    selall = p.sb("selall", [128, 8 * NT])
    scales = p.sb("scales", [128, 8 * NT])
    amb = p.sb("amb", [128, 8])
    csb = p.sb("csb", [128, 8])
    ones8 = p.sb("ones8", [128, 8])
    ones_bf = p.sb("ones_bf", [1, 128], BF16)
    rw_sb = p.sb("rw_sb", [128, 64], BF16)
    rb_sb = p.sb("rb_sb", [1, 8], BF16)
    eb_sb = p.sb("eb_sb", [1, 8 * D], BF16)
    id_bf = p.sb("id_bf", [128, 128], BF16)
    w_sb = [p.sb(f"w{i}", [128, 8 * D], BF16) for i in range(2)]

    ptr = p.ps("ptr", [128, D], BF16)
    pr = [p.ps(f"pr{i}", [128, 8]) for i in range(2)]
    py = [p.ps(f"py{i}", [128, D]) for i in range(2)]

    psem = p.sem("psem", dma=True)
    s_idl = p.sem("sidl", dma=True)
    wload = [p.sem("wl0", dma=True), p.sem("wl1", dma=True)]
    s_init = p.sem("init")
    s_norm = p.sem("nrm")
    s_tp = p.sem("tp")
    s_ev = p.sem("ev")
    s_rt = p.sem("rt")
    s_nm = p.sem("nm")
    s_ex = p.sem("ex")
    s_sel = p.sem("sel")
    s_scl = p.sem("scl")
    s_mm = p.sem("mm")
    s_stt = p.sem("stt")

    @p.on("gpsimd")
    def _(g):
        g.dma_start(out=id_bf[:, :], in_=ident).then_inc(s_idl.h, 16)
        g.dma_start(out=rw_sb[:, :], in_=rw).then_inc(psem.h, 16)
        g.dma_start(out=rb_sb[:, :], in_=rb).then_inc(psem.h, 16)
        g.dma_start(out=eb_sb[:, :], in_=eb).then_inc(psem.h, 16)
        for e in range(8):
            if e >= 2:
                WG(g, s_mm, 16 * (e - 1))
            g.dma_start(out=w_sb[e % 2][:, :], in_=ew[e, :, :]).then_inc(wload[e % 2].h, 16)

    rstd, negm, s_rstd = emit_ln(p, sx, gate, "ln")

    @p.on("scalar")
    def _(s):
        for t in range(NT):
            WG(s, s_nm, t + 1)
            p.sop(s, "activation", probs[:, 8 * t:8 * (t + 1)], pr[t % 2][:, :], AF.Exp,
                  bias=negM[:, t:t + 1], accum_out=sumall[:, t:t + 1], inc=s_ex)

    @p.on("tensor")
    def _(te):
        WG(te, s_init, 1)
        WG(te, s_idl, 16)
        WG(te, psem, 48)
        for t in range(NT):
            WG(te, s_norm, t + 1)
            if t >= 1:
                WG(te, s_ev, t)
            for fc in range(8):
                ins = te.transpose(ptr[:, fc * 128:(fc + 1) * 128],
                                   xln[:, fc * 128:(fc + 1) * 128], id_bf[:, :])
            ins.then_inc(s_tp.h, 1)
            WG(te, s_ev, t + 1)
            if t >= 2:
                WG(te, s_nm, t - 1)
                WG(te, s_ex, t - 1)
            for kc in range(8):
                te.matmul(pr[t % 2][:, :],
                          xlnT[:, (t * 8 + kc) * 128:(t * 8 + kc + 1) * 128],
                          rw_sb[:, kc * 8:(kc + 1) * 8],
                          start=(kc == 0), stop=(kc == 7))
            te.matmul(pr[t % 2][:, :], ones_bf[:, :], rb_sb[:, :],
                      start=False, stop=True,
                      skip_group_check=True).then_inc(s_rt.h, 1)
    s_tp.v += NT
    s_rt.v += NT

    @p.on("vector")
    def _(v):
        p.vop(v, "memset", ones8[:, :], 1.0)
        p.vop(v, "memset", ones_bf[:, :], 1.0, inc=s_init)
        WG(v, s_rstd, 1)
        for t in range(NT):
            if t >= 1:
                WG(v, s_tp, t)
            p.vop(v, "tensor_scalar", xln[:, :], sx[:, t * D:(t + 1) * D],
                  negm[:, t:t + 1], rstd[:, t:t + 1], ALU.add, ALU.mult, inc=s_norm)
            WG(v, s_tp, t + 1)
            p.vop(v, "tensor_copy", xlnT[:, t * D:(t + 1) * D], ptr[:, :], inc=s_ev)
            WG(v, s_rt, t + 1)
            p.vop(v, "tensor_reduce", negM[:, t:t + 1], pr[t % 2][:, :], AX, ALU.max,
                  negate=True, inc=s_nm)
        for t in range(NT):
            WG(v, s_ex, t + 1)
            p.vop(v, "tensor_scalar", amb[:, :], probs[:, 8 * t:8 * (t + 1)], 1.0,
                  None, ALU.is_ge)
            p.vop(v, "tensor_tensor_scan", csb[:, :], ones8[:, :], amb[:, :], 0.0,
                  ALU.mult, ALU.add)
            p.vop(v, "scalar_tensor_tensor", selall[:, 8 * t:8 * (t + 1)], csb[:, :],
                  1.0, amb[:, :], ALU.is_equal, ALU.mult, inc=s_sel)
        p.vop(v, "reciprocal", ghalf[:, :], sumall[:, :])
        p.vop(v, "tensor_scalar", ghalf2[:, :], ghalf[:, :], 0.5, None, ALU.mult)
        for t in range(NT):
            p.vop(v, "tensor_scalar", scales[:, 8 * t:8 * (t + 1)],
                  selall[:, 8 * t:8 * (t + 1)], ghalf2[:, t:t + 1], None, ALU.mult,
                  inc=(s_scl if t == NT - 1 else None))

    @p.on("tensor")
    def _(te):
        for e in range(8):
            WG(te, wload[e % 2], 16 * (e // 2 + 1))
            for t in range(NT):
                k = e * NT + t
                if k >= 2:
                    WG(te, s_stt, k - 1)
                for kc in range(8):
                    for h in range(2):
                        te.matmul(py[k % 2][:, h * 512:(h + 1) * 512],
                                  xlnT[:, (t * 8 + kc) * 128:(t * 8 + kc + 1) * 128],
                                  w_sb[e % 2][:, kc * D + h * 512:kc * D + (h + 1) * 512],
                                  start=(kc == 0), stop=(kc == 7))
                te.matmul(py[k % 2][:, 0:512], ones_bf[:, :],
                          eb_sb[:, e * D:e * D + 512], start=False, stop=True,
                          skip_group_check=True)
                te.matmul(py[k % 2][:, 512:1024], ones_bf[:, :],
                          eb_sb[:, e * D + 512:(e + 1) * D],
                          start=False, stop=True,
                          skip_group_check=True).then_inc(s_mm.h, 1)
    s_mm.v += 8 * NT

    @p.on("vector")
    def _(v):
        WG(v, s_scl, 1)
        for e in range(8):
            for t in range(NT):
                k = e * NT + t
                WG(v, s_mm, k + 1)
                p.vop(v, "scalar_tensor_tensor", sx[:, t * D:(t + 1) * D],
                      py[k % 2][:, :], scales[:, t * 8 + e:t * 8 + e + 1],
                      sx[:, t * D:(t + 1) * D], ALU.mult, ALU.add, inc=s_stt)

    s_idl.v = 16
    psem.v = 48
    wload[0].v = 64
    wload[1].v = 64
    return s_stt, 8 * NT


def moe_aps(nc):
    return {
        "rw_dev": nc.dram_tensor("rw_dev", [128, 64], F32, kind="ExternalInput").ap(),
        "rb": nc.dram_tensor("rb", [1, 8], F32, kind="ExternalInput").ap(),
        "ew_dev": nc.dram_tensor("ew_dev", [8, 128, 8 * D], F32, kind="ExternalInput").ap(),
        "eb_row": nc.dram_tensor("eb_row", [1, 8 * D], F32, kind="ExternalInput").ap(),
    }


def _standalone(emitter, aps_fn, stage_extra=None):
    nc = bass.Bass("TRN2", target_bir_lowering=False)
    x = nc.dram_tensor("x", [T, D], F32, kind="ExternalInput").ap()
    xo = nc.dram_tensor("xo", [T, D], F32, kind="ExternalOutput").ap()
    A = aps_fn(nc)
    A["ident"] = nc.dram_tensor("ident", [128, 128], F32, kind="ExternalInput").ap()
    with contextlib.ExitStack() as es:
        p = Prog(nc, es)
        sx = p.sb("sx", [128, NT * D])
        gdsem = p.sem("gdsem")

        @p.on("sync")
        def _(sy):
            for t in range(NT):
                sy.dma_start(out=sx[:, t * D:(t + 1) * D],
                             in_=x[t * 128:(t + 1) * 128, :]).then_inc(gdsem.h, 16)

        p.begin_block("b0")
        done, dtotal = emitter(p, A, sx, (gdsem, 16 * NT))
        p.end_block()

        @p.on("sync")
        def _(sy):
            WG(sy, done, dtotal)
            for t in range(NT):
                sy.dma_start(out=xo[t * 128:(t + 1) * 128, :],
                             in_=sx[:, t * D:(t + 1) * D]).then_inc(gdsem.h, 16)
            WG(sy, gdsem, 16 * NT + 16 * NT)

        p.finalize()
    return nc


def build_moe(dbg=False):
    return _standalone(emit_moe, moe_aps)



def emit_mamba(p, A, sx, gate, phases=3, p2dt=16):
    NDT = 16
    dbg = False
    dbgt = {}
    w_in = A["w_in_dev"]
    w_cv = A["w_conv_dev"]
    w_xp = A["w_xproj_dev"]
    w_dt = A["w_dt_dev"]
    alog = A["alog_dev"]
    bdt = A["bdt_dev"]
    dd = A["dd_dev"]
    w_out = A["w_out_dev"]
    seld = A["sel_dev"]
    ident = A["ident"]
    hs_scr = A["hs_scr"]
    yt_scr = A["yt_scr"]
    xlnT = p.sb("xlnT", [128, 8 * T], BF16)
    xln = p.sb("xln0", [128, D], BF16)
    big = [p.sb(f"big{i}", [128, 2048]) for i in range(7)]
    big7 = p.sb("big7", [128, 2048], BF16)
    dAb = [None, None]
    xdbl_bf = p.sb("xdbl_bf", [128, 2048], BF16)
    shs = [p.sb(f"shs{i}", [128, 2048], BF16) for i in range(2)]
    dbxb2 = p.sb("dbxb2", [128, 2048], BF16)
    syt = [p.sb(f"syt{i}", [128, 2048], BF16) for i in range(2)]
    win_sb = [p.sb(f"win{i}", [128, 1024], BF16) for i in range(2)]
    wres_sb = [p.sb(f"wres{i}", [128, 1024], BF16) for i in range(2)]
    wout_sb = [p.sb(f"wout{i}", [128, 2048], BF16) for i in range(2)]
    wxp_sb = [p.sb(f"wxp{i}", [128, 128], BF16) for i in range(2)]

    wcv_sb = p.sb("wcv_sb", [128, 64])
    sel_sb = p.sb("sel_sb", [128, 2048], BF16)
    wdt_sb = sel_sb
    alog_sb = p.sb("alog_sb", [128, 256])
    aneg_sb = alog_sb
    bdt_sb = p.sb("bdt_sb", [128, 16])
    dd_sb = p.sb("dd_sb", [128, 16])
    id_bf = p.sb("id_bf", [128, 128], BF16)

    psem = p.sem("psem", dma=True)
    s_id = p.sem("sid", dma=True)
    s_wl = [p.sem("wl0", dma=True), p.sem("wl1", dma=True)]
    s_xl = [p.sem("xl0", dma=True), p.sem("xl1", dma=True)]
    s_wr = [p.sem("wr0", dma=True), p.sem("wr1", dma=True)]
    s_wo = [p.sem("wo0", dma=True), p.sem("wo1", dma=True)]
    s_hso = [p.sem("hso0", dma=True), p.sem("hso1", dma=True)]
    s_hsi = [p.sem("hsi0", dma=True), p.sem("hsi1", dma=True)]
    s_yto = [p.sem("yto0", dma=True), p.sem("yto1", dma=True)]
    s_yti = [p.sem(f"yti{i}", dma=True) for i in range(2)]
    s_n0 = p.sem("n0")
    s_tp0 = p.sem("tp0")
    s_ev0 = p.sem("ev0")
    s_hmm = p.sem("hmm")
    s_conv = p.sem("conv")
    s_silu1 = p.sem("silu1")
    s_xpj = p.sem("xpj")
    s_xadd = p.sem("xadd")
    s_xbf = p.sem("xbf")
    s_Aexp = p.sem("Aexp")
    s_Aneg = p.sem("Aneg")
    s_dtp = p.sem("dtp")
    s_pacc = p.sem("pacc")
    s_dts = p.sem("dts")
    s_delta = p.sem("delta")
    s_dA = p.sem("dA")
    s_scan = p.sem("scan")
    s_repB = p.sem("repB")
    s_repC = p.sem("repC")
    s_dbx = p.sem("dbx")
    s_prod = p.sem("prod")
    s_res = p.sem("res")
    s_silu = p.sem("silu")
    s_ep = p.sem("ep")
    s_ommf = p.sem("ommf")
    s_oevc = p.sem("oevc")
    s_otp = p.sem("otp")
    s_ores = p.sem("ores")

    ptr0, ptr0_cm = p.ps2("ptr0", [128, D], BF16)

    @p.on("gpsimd")
    def _(g):
        g.dma_start(out=id_bf[:, :], in_=ident).then_inc(s_id.h, 16)
        g.dma_start(out=wdt_sb[64:128, :], in_=w_dt).then_inc(psem.h, 16)
        g.dma_start(out=sel_sb[0:48, :], in_=seld).then_inc(psem.h, 16)
        g.dma_start(out=alog_sb[:, :], in_=alog).then_inc(psem.h, 16)
        g.dma_start(out=bdt_sb[:, :], in_=bdt).then_inc(psem.h, 16)
        g.dma_start(out=dd_sb[:, :], in_=dd).then_inc(psem.h, 16)
        g.dma_start(out=wcv_sb[:, :].rearrange("p (k c) -> p k c", k=16),
                    in_=w_cv.rearrange("k p c -> p k c")).then_inc(psem.h, 16)
        for dt in range(NDT if phases >= 1 else 0):
            if dt >= 2:
                WG(g, s_hmm, dt - 1)
            g.dma_start(out=win_sb[dt % 2][:, :], in_=w_in[dt, :, :]).then_inc(s_wl[dt % 2].h, 16)
            if dt >= 2:
                WG(g, s_xpj, dt - 1)
            g.dma_start(out=wxp_sb[dt % 2][:, :], in_=w_xp[dt, :, :]).then_inc(s_xl[dt % 2].h, 16)


    rstd, negm, s_rstd = emit_ln(p, sx, gate, "ln", sq_buf=big[0])

    @p.on("vector")
    def _(v):
        WG(v, s_rstd, 1)
        for t in range(NT):
            if t >= 1:
                WG(v, s_tp0, t)
            p.vop(v, "tensor_scalar", xln[:, :], sx[:, t * D:(t + 1) * D],
                  negm[:, t:t + 1], rstd[:, t:t + 1], ALU.add, ALU.mult, inc=s_n0)
            WG(v, s_tp0, t + 1)
            for fc in range(8):
                p.vop(v, "tensor_copy",
                      xlnT[:, fc * T + t * 128:fc * T + (t + 1) * 128],
                      ptr0[:, fc * 128:(fc + 1) * 128],
                      inc=(s_ev0 if fc == 7 else None))

    @p.on("tensor")
    def _(te):
        WG(te, s_id, 16)
        for t in range(NT):
            WG(te, s_n0, t + 1)
            if t >= 1:
                WG(te, s_ev0, t)
            for fc in range(8):
                ins = te.transpose(ptr0[:, fc * 128:(fc + 1) * 128],
                                   xln[:, fc * 128:(fc + 1) * 128], id_bf[:, :])
            ins.then_inc(s_tp0.h, 1)
    s_tp0.v += NT

    ptr0_cm.__exit__(None, None, None)
    psum_h, psum_h_cm = p.ps2("psum_h", [128, 2048])
    psum_x, psum_x_cm = p.ps2("psum_x", [128, 2048])

    @p.on_if(phases >= 1, "tensor")
    def _(te):
        WG(te, s_ev0, NT)
        WG(te, psem, 96)
        for dt in range(NDT):
            WG(te, s_wl[dt % 2], 16 * (dt // 2 + 1))
            if dt >= 1:
                WG(te, s_conv, dt)
            for nh in range(4):
                for kc in range(8):
                    ins = te.matmul(psum_h[:, nh * 512:(nh + 1) * 512],
                                    win_sb[dt % 2][:, kc * 128:(kc + 1) * 128],
                                    xlnT[:, kc * T + nh * 512:kc * T + (nh + 1) * 512],
                                    start=(kc == 0), stop=(kc == 7))
            ins.then_inc(s_hmm.h, 1)
            WG(te, s_silu1, dt + 1)
            WG(te, s_xl[dt % 2], 16 * (dt // 2 + 1))
            if dt >= 1:
                WG(te, s_xadd, dt)
            for nh in range(4):
                ins = te.matmul(psum_x[:, nh * 512:(nh + 1) * 512],
                                wxp_sb[dt % 2][:, :],
                                shs[dt % 2][:, nh * 512:(nh + 1) * 512],
                                start=True, stop=True)
            ins.then_inc(s_xpj.h, 1)
    s_hmm.v += NDT
    s_xpj.v += NDT

    @p.on_if(phases >= 1, "vector")
    def _(v):
        for dt in range(NDT):
            WG(v, s_hmm, dt + 1)
            if dt >= 1:
                WG(v, s_silu1, dt)
            p.vop(v, "tensor_scalar", big[0][:, :], psum_h[:, :],
                  wcv_sb[:, dt * 4 + 3:dt * 4 + 4], None, ALU.mult)
            for k in (2, 1, 0):
                sft = 3 - k
                p.vop(v, "scalar_tensor_tensor", big[0][:, sft:2048],
                      psum_h[:, 0:2048 - sft], wcv_sb[:, dt * 4 + k:dt * 4 + k + 1],
                      big[0][:, sft:2048], ALU.mult, ALU.add,
                      inc=(s_conv if k == 0 else None))
            WG(v, s_xpj, dt + 1)
            if dt == 0:
                p.vop(v, "tensor_scalar", big[6][:, :], psum_x[:, :], 1.0, None,
                      ALU.mult, inc=s_xadd)
            else:
                p.vop(v, "scalar_tensor_tensor", big[6][:, :], psum_x[:, :], 1.0,
                      big[6][:, :], ALU.mult, ALU.add, inc=s_xadd)
        p.vop(v, "tensor_scalar", xdbl_bf[:, :], big[6][:, :], 1.0, None, ALU.mult,
              inc=s_xbf)
        WG(v, s_Aexp, 1)
        p.vop(v, "tensor_scalar", aneg_sb[:, :], alog_sb[:, :], -1.0, None, ALU.mult,
              inc=s_Aneg)

    @p.on_if(phases >= 1, "scalar")
    def _(s):
        for dt in range(NDT):
            WG(s, s_conv, dt + 1)
            if dt >= 2:
                WG(s, s_xpj, dt - 1)
                WG(s, s_hso[dt % 2], 16 * (dt // 2))
            p.sop(s, "activation", shs[dt % 2][:, :], big[0][:, :], AF.Silu,
                  inc=s_silu1)
        WG(s, psem, 96)
        p.sop(s, "activation", alog_sb[:, :], alog_sb[:, :], AF.Exp, inc=s_Aexp)

    @p.on_if(phases >= 1, "sync")
    def _(sy):
        for dt in range(NDT):
            WG(sy, s_silu1, dt + 1)
            sy.dma_start(out=hs_scr[dt, :, :], in_=shs[dt % 2][:, :]).then_inc(s_hso[dt % 2].h, 16)

    psum_x_cm.__exit__(None, None, None)
    psum_h_cm.__exit__(None, None, None)
    psum_d, psum_d_cm = p.ps2("psum_d", [128, 2048])
    psum_r, psum_r_cm = p.ps2("psum_r", [128, 2048])
    prr = [psum_r, psum_d]
    dAb[0] = big[3]
    dAb[1] = big[6]

    @p.on_if(phases >= 2, "sync")
    def _(sy):
        WG(sy, s_hso[0], 16 * 8)
        WG(sy, s_hso[1], 16 * 8)
        WG(sy, s_xpj, 16)
        for dt in range(p2dt):
            if dt >= 2:
                WG(sy, s_ep, dt - 1)
            sy.dma_start(out=shs[dt % 2][:, :], in_=hs_scr[dt, :, :]).then_inc(s_hsi[dt % 2].h, 16)
            if dt >= 1:
                WG(sy, s_ep, dt)
                sy.dma_start(out=yt_scr[dt - 1, :, :],
                             in_=syt[(dt - 1) % 2][:, :]).then_inc(s_yto[(dt - 1) % 2].h, 16)
        WG(sy, s_ep, p2dt)
        sy.dma_start(out=yt_scr[p2dt - 1, :, :],
                     in_=syt[(p2dt - 1) % 2][:, :]).then_inc(s_yto[(p2dt - 1) % 2].h, 16)

    @p.on_if(phases >= 2, "tensor")
    def _(te):
        WG(te, s_xbf, 1)
        for dt in range(p2dt):
            if dt >= 1:
                WG(te, s_silu, dt)
                WG(te, s_prod, 16 * dt)
            for nh in range(4):
                ins = te.matmul(psum_d[:, nh * 512:(nh + 1) * 512],
                                wdt_sb[64:128, dt * 128:(dt + 1) * 128],
                                xdbl_bf[64:128, nh * 512:(nh + 1) * 512],
                                start=True, stop=True)
            ins.then_inc(s_dtp.h, 1)
            for n in range(16):
                k = dt * 16 + n
                b = prr[n % 2]
                if n == 1:
                    WG(te, s_dts, dt + 1)
                if n >= 2:
                    WG(te, s_prod, k - 1)
                for nh in range(4):
                    ins = te.matmul(b[:, nh * 512:(nh + 1) * 512],
                                    sel_sb[0:16, n * 128:(n + 1) * 128],
                                    xdbl_bf[0:16, nh * 512:(nh + 1) * 512],
                                    start=True, stop=True)
                ins.then_inc(s_repB.h, 1)
                WG(te, s_dbx, k + 1)
                for nh in range(4):
                    ins = te.matmul(b[:, nh * 512:(nh + 1) * 512],
                                    sel_sb[32:48, n * 128:(n + 1) * 128],
                                    xdbl_bf[32:48, nh * 512:(nh + 1) * 512],
                                    start=True, stop=True)
                ins.then_inc(s_repC.h, 1)
            WG(te, s_dts, dt + 1)
            WG(te, s_prod, 16 * (dt + 1))
            WG(te, s_wr[dt % 2], 16 * (dt // 2 + 1))
            for nh in range(4):
                for kc in range(8):
                    ins = te.matmul(psum_d[:, nh * 512:(nh + 1) * 512],
                                    wres_sb[dt % 2][:, kc * 128:(kc + 1) * 128],
                                    xlnT[:, kc * T + nh * 512:kc * T + (nh + 1) * 512],
                                    start=(kc == 0), stop=(kc == 7))
            ins.then_inc(s_res.h, 1)

    @p.on_if(phases >= 2, "vector")
    def _(v):
        for dt in range(p2dt):
            WG(v, s_dtp, dt + 1)
            if dt >= 1:
                WG(v, s_delta, dt)
            p.vop(v, "tensor_scalar", big[0][:, :], psum_d[:, :],
                  bdt_sb[:, dt:dt + 1], 80.0, ALU.add, ALU.min, inc=s_dts)
            WG(v, s_delta, dt + 1)
            WG(v, s_hsi[dt % 2], 16 * (dt // 2 + 1))
            p.vop(v, "tensor_tensor", big[2][:, :], big[1][:, :], shs[dt % 2][:, :],
                  ALU.mult)
            for n in range(16):
                k = dt * 16 + n
                WG(v, s_repB, k + 1)
                p.vop(v, "tensor_tensor", dbxb2[:, :], big[2][:, :],
                      prr[n % 2][:, :], ALU.mult, inc=s_dbx)
                WG(v, s_dA, k + 1)
                p.vop(v, "tensor_tensor_scan", big[4][:, :], dAb[k % 2][:, :],
                      dbxb2[:, :], 0.0, ALU.mult, ALU.add, inc=s_scan)
                WG(v, s_repC, k + 1)
                if n == 0:
                    WG(v, s_pacc, 15 * dt)
                    p.vop(v, "tensor_tensor", big[5][:, :], big[4][:, :],
                          prr[n % 2][:, :], ALU.mult, inc=s_prod)
                else:
                    WG(v, s_pacc, 15 * dt + n - 1)
                    p.vop(v, "tensor_tensor", big7[:, :], big[4][:, :],
                          prr[n % 2][:, :], ALU.mult, inc=s_prod)
            WG(v, s_pacc, 15 * (dt + 1))
            p.vop(v, "scalar_tensor_tensor", big[2][:, :], shs[dt % 2][:, :],
                  dd_sb[:, dt:dt + 1], big[5][:, :], ALU.mult, ALU.add)
            WG(v, s_silu, dt + 1)
            WG(v, s_yto[dt % 2], 16 * (dt // 2))
            p.vop(v, "tensor_tensor", syt[dt % 2][:, :], big[2][:, :],
                  syt[dt % 2][:, :], ALU.mult, inc=s_ep)

    @p.on_if(phases >= 2, "gpsimd")
    def _(g):
        for dt in range(p2dt):
            if dt >= 2:
                WG(g, s_res, dt - 1)
            g.dma_start(out=wres_sb[dt % 2][:, :],
                        in_=w_in[16 + dt, :, :]).then_inc(s_wr[dt % 2].h, 16)
            for n in range(1, 16):
                k = dt * 16 + n
                WG(g, s_prod, k + 1)
                p.gop(g, "tensor_tensor", big[5][:, :], big[5][:, :],
                      big7[:, :], ALU.add, inc=s_pacc)
            if phases >= 3 and dt in (12, 14):
                m = (dt - 12) // 2
                g.dma_start(out=wout_sb[m % 2][:, :],
                            in_=w_out[m, :, :]).then_inc(s_wo[m % 2].h, 16)

    @p.on_if(phases >= 3, "gpsimd")
    def _(g):
        for m in range(2, 8):
            WG(g, s_ommf, NDT * (m - 1))
            g.dma_start(out=wout_sb[m % 2][:, :],
                        in_=w_out[m, :, :]).then_inc(s_wo[m % 2].h, 16)

    @p.on_if(phases >= 2, "scalar")
    def _(s):
        WG(s, s_Aneg, 1)
        for dt in range(p2dt):
            WG(s, s_dts, dt + 1)
            if dt >= 1:
                WG(s, s_ep, dt)
            p.sop(s, "activation", big[0][:, :], big[0][:, :], AF.Exp)
            p.sop(s, "activation", big[1][:, :], big[0][:, :], AF.Ln, bias=1.0,
                  inc=s_delta)
            for n in range(16):
                k = dt * 16 + n
                if k >= 1:
                    WG(s, s_scan, k - 1)
                p.sop(s, "activation", dAb[k % 2][:, :], big[1][:, :], AF.Exp,
                      scale=aneg_sb[:, dt * 16 + n:dt * 16 + n + 1], inc=s_dA)
            WG(s, s_res, dt + 1)
            WG(s, s_yto[dt % 2], 16 * (dt // 2))
            p.sop(s, "activation", syt[dt % 2][:, :], psum_d[:, :], AF.Silu,
                  inc=s_silu)

    psum_r_cm.__exit__(None, None, None)
    psum_d_cm.__exit__(None, None, None)
    psum_o, psum_o_cm = p.ps2("psum_o", [128, 2048])
    ptr2, ptr2_cm = p.ps2("ptr2", [128, 2048], BF16)
    syt4 = syt
    outT_sb = shs[0]

    @p.on_if(phases >= 3, "sync")
    def _(sy):
        WG(sy, s_yto[0], 16 * 8)
        WG(sy, s_yto[1], 16 * 8)
        for m in range(8):
            for dt in range(NDT):
                j = m * NDT + dt
                sl = j % 2
                if j >= 2:
                    WG(sy, s_ommf, j - 1)
                sy.dma_start(out=syt4[sl][:, :], in_=yt_scr[dt, :, :]).then_inc(s_yti[sl].h, 16)

    @p.on_if(phases >= 3, "tensor")
    def _(te):
        for m in range(8):
            WG(te, s_wo[m % 2], 16 * (m // 2 + 1))
            if m >= 1:
                WG(te, s_oevc, m)
            for dt in range(NDT):
                j = m * NDT + dt
                sl = j % 2
                WG(te, s_yti[sl], 16 * (j // 2 + 1))
                for nh in range(4):
                    ins = te.matmul(psum_o[:, nh * 512:(nh + 1) * 512],
                                    wout_sb[m % 2][:, dt * 128:(dt + 1) * 128],
                                    syt4[sl][:, nh * 512:(nh + 1) * 512],
                                    start=(dt == 0), stop=(dt == NDT - 1))
                ins.then_inc(s_ommf.h, 1)
            WG(te, s_oevc, m + 1)
            if m >= 1:
                WG(te, s_ores, 16 * m)
            for t in range(NT):
                ins = te.transpose(ptr2[:, t * 128:(t + 1) * 128],
                                   outT_sb[:, t * 128:(t + 1) * 128], id_bf[:, :])
            ins.then_inc(s_otp.h, 1)

    @p.on_if(phases >= 3, "scalar")
    def _(s):
        for m in range(8):
            WG(s, s_ommf, NDT * (m + 1))
            if m >= 1:
                WG(s, s_otp, m)
            if m == 0:
                WG(s, s_ep, NDT)
            p.sop(s, "activation", outT_sb[:, :], psum_o[:, :], AF.Copy, inc=s_oevc)

    @p.on_if(phases >= 3, "vector")
    def _(v):
        for m in range(8):
            WG(v, s_otp, m + 1)
            for t in range(NT):
                p.vop(v, "scalar_tensor_tensor",
                      sx[:, t * D + m * 128:t * D + (m + 1) * 128],
                      ptr2[:, t * 128:(t + 1) * 128], 1.0,
                      sx[:, t * D + m * 128:t * D + (m + 1) * 128],
                      ALU.mult, ALU.add, inc=s_ores)


    ptr2_cm.__exit__(None, None, None)
    psum_o_cm.__exit__(None, None, None)
    s_id.v = 16
    psem.v = 96
    for q in (s_wl, s_xl, s_wr, s_hso, s_hsi, s_yto):
        q[0].v = 128
        q[1].v = 128
    s_wo[0].v = 64
    s_wo[1].v = 64
    s_yti[0].v = 1024
    s_yti[1].v = 1024
    s_dtp.v = 16
    s_repB.v = 256
    s_repC.v = 256
    s_res.v = 16
    s_ommf.v = 128
    s_otp.v = 8
    return s_ores, 8 * NT


def mamba_aps(nc):
    return {
        "w_in_dev": nc.dram_tensor("w_in_dev", [32, 128, 1024], F32, kind="ExternalInput").ap(),
        "w_conv_dev": nc.dram_tensor("w_conv_dev", [16, 128, 4], F32, kind="ExternalInput").ap(),
        "w_xproj_dev": nc.dram_tensor("w_xproj_dev", [16, 128, 128], F32, kind="ExternalInput").ap(),
        "w_dt_dev": nc.dram_tensor("w_dt_dev", [64, 2048], F32, kind="ExternalInput").ap(),
        "alog_dev": nc.dram_tensor("alog_dev", [128, 256], F32, kind="ExternalInput").ap(),
        "bdt_dev": nc.dram_tensor("bdt_dev", [128, 16], F32, kind="ExternalInput").ap(),
        "dd_dev": nc.dram_tensor("dd_dev", [128, 16], F32, kind="ExternalInput").ap(),
        "w_out_dev": nc.dram_tensor("w_out_dev", [8, 128, 2048], F32, kind="ExternalInput").ap(),
        "sel_dev": nc.dram_tensor("sel_dev", [48, 2048], F32, kind="ExternalInput").ap(),
        "hs_scr": nc.dram_tensor("hs_scr", [16, 128, 2048], BF16).ap(),
        "yt_scr": nc.dram_tensor("yt_scr", [16, 128, 2048], BF16).ap(),
    }


def build_mamba(dbg=False, phases=3, p2dt=16):
    return _standalone(emit_mamba, mamba_aps)



def _stage_moe(inputs):
    rw = inputs["L0_ffn_router_w"]
    rb = inputs["L0_ffn_router_b"]
    ew = inputs["L0_ffn_expert_w"]
    eb = inputs["L0_ffn_expert_b"]
    rw_dev = np.ascontiguousarray(
        rw.reshape(8, 128, 8).transpose(1, 0, 2).reshape(128, 64))
    ew_dev = np.ascontiguousarray(
        ew.reshape(8, 8, 128, D).transpose(0, 2, 1, 3).reshape(8, 128, 8 * D))
    return {
        "rw_dev": rw_dev,
        "rb": np.ascontiguousarray(rb.reshape(1, 8)),
        "ew_dev": ew_dev,
        "eb_row": np.ascontiguousarray(eb.reshape(1, 8 * D)),
        "ident": np.eye(128, dtype=np.float32),
    }


def _stage_mamba(inputs):
    w_in = inputs["L0_attn_w_in"]
    w_conv = inputs["L0_attn_w_conv"]
    w_xproj = inputs["L0_attn_w_xproj"]
    w_dtw = inputs["L0_attn_w_dt"]
    b_dt = inputs["L0_attn_b_dt"]
    a_log = inputs["L0_attn_A_log"]
    dvec = inputs["L0_attn_D"]
    w_outw = inputs["L0_attn_w_out"]
    c = np.ascontiguousarray
    w_in_dev = c(w_in.reshape(32, 128, 8, 128).transpose(0, 3, 2, 1).reshape(32, 128, 1024))
    w_conv_dev = c(w_conv.reshape(16, 128, 4))
    wxT = w_xproj.T
    wx = np.zeros((2048, 128), np.float32)
    wx[:, 0:16] = wxT[:, 64:80]
    wx[:, 32:48] = wxT[:, 80:96]
    wx[:, 64:128] = wxT[:, 0:64]
    w_xproj_dev = c(wx.reshape(16, 128, 128))
    w_dt_dev = c(w_dtw.T)
    alog_dev = c(a_log.reshape(16, 128, 16).transpose(1, 0, 2).reshape(128, 256))
    bdt_dev = c(b_dt.reshape(16, 128).T.reshape(128, 16))
    dd_dev = c(dvec.reshape(16, 128).T.reshape(128, 16))
    w_out_dev = c(w_outw.T.reshape(16, 128, 8, 128).transpose(2, 1, 0, 3).reshape(8, 128, 2048))
    sel1 = np.zeros((16, 16, 128), np.float32)
    for n in range(16):
        sel1[n, n, :] = 1.0
    sel1 = sel1.transpose(1, 0, 2).reshape(16, 2048)
    sel_dev = np.zeros((48, 2048), np.float32)
    sel_dev[0:16] = sel1
    sel_dev[32:48] = sel1
    sel_dev = c(sel_dev)
    return {
        "w_in_dev": w_in_dev, "w_conv_dev": w_conv_dev, "w_xproj_dev": w_xproj_dev,
        "w_dt_dev": w_dt_dev, "alog_dev": alog_dev, "bdt_dev": bdt_dev,
        "dd_dev": dd_dev, "w_out_dev": w_out_dev, "sel_dev": sel_dev,
        "ident": np.eye(128, dtype=np.float32),
    }



def emit_attn(p, A, sx, gate):
    NC = 16
    SCALE = 1.0 / 32.0
    wq = A["wq_dev"]
    wk = A["wk_dev"]
    wv = A["wv_dev"]
    wo = A["wo_dev"]
    bq = A["bq_row"]
    bk = A["bk_row"]
    bv = A["bv_cols"]
    bo = A["bo_row"]
    ident = A["ident"]
    scr = A["ascr"]
    xlnT = p.sb("xlnT", [128, NT * D], BF16)
    xln = p.sb("xln0", [128, D], BF16)
    qall = p.sb("qall", [128, D], BF16)
    kdup = p.sb("kdup", [128, 2 * D], BF16)
    vt = p.sb("vt", [128, D], BF16)
    psb_ = p.sb("psb", [128, 2048], BF16)
    ptsb = p.sb("ptsb", [128, 2048], BF16)
    pvall = p.sb("pvall", [128, D], BF16)
    a_tm = p.sb("a_tm", [128, D], BF16)
    a_fm = p.sb("a_fm", [128, D], BF16)
    negMall = p.sb("negMall", [128, 16])
    rsum = p.sb("rsum", [128, 16])
    rsinv = p.sb("rsinv", [128, 16])
    ones_bf = p.sb("ones_bf", [1, 128], BF16)
    wq_sb = p.sb("wq_sb", [128, 8 * D], BF16)
    wk_sb = p.sb("wk_sb", [128, 8 * D], BF16)
    wv_sb = p.sb("wv_sb", [128, 8 * D], BF16)
    wo_sb = p.sb("wo_sb", [128, 8 * D], BF16)
    bq_sb = p.sb("bq_sb", [1, D], BF16)
    bk_sb = p.sb("bk_sb", [1, D], BF16)
    bv_sb = p.sb("bv_sb", [128, 8])
    bo_sb = p.sb("bo_sb", [1, D], BF16)
    id_bf = p.sb("id_bf", [128, 128], BF16)

    psA = p.ps("psA", [128, 2048])
    psB = p.ps("psB", [128, D])
    psC = p.ps("psC", [128, D], BF16)
    ptr0 = psC

    psem = p.sem("psem", dma=True)
    s_id = p.sem("sid", dma=True)
    s_n0 = p.sem("n0")
    s_tp0 = p.sem("tp0")
    s_ev0 = p.sem("ev0")
    sev = [p.sb("sev0", [128, 2048]), p.sb("sev1", [128, 2048])]
    s_sev = p.sem("sev")
    s_qmm = p.sem("qmm")
    s_qev = p.sem("qev")
    s_kmm = p.sem("kmm")
    s_kev = p.sem("kev")
    s_vmm = p.sem("vmm")
    s_vev = p.sem("vev")
    s_smm = p.sem("smm")
    s_snm = p.sem("snm")
    s_pex = p.sem("pex")
    s_ptp = p.sem("ptp")
    s_ptev = p.sem("ptev")
    s_pvmm = p.sem("pvmm")
    s_pvev = p.sem("pvev")
    s_rr1 = [p.sem("rr1a", dma=True), p.sem("rr1b", dma=True)]
    s_rr2 = p.sem("rr2", dma=True)
    s_atp = p.sem("atp")
    s_afev = p.sem("afev")
    s_omm2 = p.sem("omm2")
    s_ares = p.sem("ares")

    @p.on("sync")
    def _(sy):
        for c in range(NC):
            WG(sy, s_pvev, 16 * (c + 1))
            if c >= 2:
                WG(sy, s_rr2, 16 * (c - 1))
            sy.dma_start(out=scr[c % 2], in_=pvall[:, :]).then_inc(s_rr1[c % 2].h, 16)
            WG(sy, s_rr1[c % 2], 16 * (c // 2 + 1))
            WG(sy, s_atp, c)
            src = bass.AP(scr[c % 2].tensor, 0,
                          [[512, 2], [64, 8], [16384, 8], [1024, 16], [1, 64]])
            sy.dma_start(out=a_tm[:, :], in_=src).then_inc(s_rr2.h, 16)

    @p.on("gpsimd")
    def _(g):
        g.dma_start(out=id_bf[:, :], in_=ident).then_inc(s_id.h, 16)
        g.dma_start(out=wq_sb[:, :], in_=wq).then_inc(psem.h, 16)
        g.dma_start(out=wk_sb[:, :], in_=wk).then_inc(psem.h, 16)
        g.dma_start(out=wv_sb[:, :], in_=wv).then_inc(psem.h, 16)
        g.dma_start(out=wo_sb[:, :], in_=wo).then_inc(psem.h, 16)
        g.dma_start(out=bq_sb[:, :], in_=bq).then_inc(psem.h, 16)
        g.dma_start(out=bk_sb[:, :], in_=bk).then_inc(psem.h, 16)
        g.dma_start(out=bv_sb[:, :], in_=bv).then_inc(psem.h, 16)
        g.dma_start(out=bo_sb[:, :], in_=bo).then_inc(psem.h, 16)

    rstd, negm, s_rstd = emit_ln(p, sx, gate, "ln", sq_buf=sev[0][:, 0:1024])

    @p.on("vector")
    def _(v):
        p.vop(v, "memset", ones_bf[:, :], 1.0)
        WG(v, s_rstd, 1)
        for t in range(NT):
            if t >= 1:
                WG(v, s_tp0, t)
            p.vop(v, "tensor_scalar", xln[:, :], sx[:, t * D:(t + 1) * D],
                  negm[:, t:t + 1], rstd[:, t:t + 1], ALU.add, ALU.mult, inc=s_n0)
            WG(v, s_tp0, t + 1)
            p.vop(v, "tensor_copy", xlnT[:, t * D:(t + 1) * D], ptr0[:, :], inc=s_ev0)
        for c in range(NC):
            if c >= 1:
                WG(v, s_rr1[(c - 1) % 2], 16 * ((c - 1) // 2 + 1))
            WG(v, s_vmm, c + 1)
            if c >= 1:
                WG(v, s_pvmm, 16 * c)
            for fc in range(8):
                p.vop(v, "tensor_scalar", vt[:, fc * 128:(fc + 1) * 128],
                      psB[:, fc * 128:(fc + 1) * 128], bv_sb[:, fc:fc + 1], None,
                      ALU.add, inc=(s_vev if fc == 7 else None))
            for rb in range(16):
                K = 16 * c + rb
                WG(v, s_sev, K + 1)
                p.vop(v, "tensor_reduce", negMall[:, rb:rb + 1], sev[K % 2][:, :], AX,
                      ALU.max, negate=True, inc=s_snm)
                for hh in range(2):
                    WG(v, s_ptp, 2 * K + hh + 1)
                    for q2 in range(2):
                        p.vop(v, "tensor_copy",
                              ptsb[:, hh * 1024 + q2 * 512:hh * 1024 + (q2 + 1) * 512],
                              psC[:, q2 * 512:(q2 + 1) * 512],
                              inc=(s_ptev if q2 == 1 else None))
                WG(v, s_pex, K + 1)
                p.vop(v, "reciprocal", rsinv[:, rb:rb + 1], rsum[:, rb:rb + 1])
                WG(v, s_pvmm, K + 1)
                p.vop(v, "tensor_scalar", pvall[:, rb * 64:(rb + 1) * 64],
                      psB[:, 0:64], rsinv[:, rb:rb + 1], None, ALU.mult, inc=s_pvev)
            WG(v, s_atp, c + 1)
            p.vop(v, "tensor_copy", a_fm[:, :], ptr0[:, :], inc=s_afev)
            WG(v, s_omm2, c + 1)
            for h in range(2):
                p.vop(v, "scalar_tensor_tensor",
                      sx[:, c * D + h * 512:c * D + (h + 1) * 512],
                      psB[:, h * 512:(h + 1) * 512], 1.0,
                      sx[:, c * D + h * 512:c * D + (h + 1) * 512],
                      ALU.mult, ALU.add, inc=s_ares)

    @p.on("scalar")
    def _(s):
        for c in range(NC):
            WG(s, s_qmm, c + 1)
            if c >= 1:
                WG(s, s_smm, 16 * c)
            p.sop(s, "activation", qall[0:64, :], psB[0:64, :], AF.Copy, scale=SCALE)
            p.sop(s, "activation", qall[64:128, :], psB[64:128, :], AF.Copy,
                  scale=SCALE, inc=s_qev)
            WG(s, s_kmm, c + 1)
            p.sop(s, "activation", kdup[:, :], psA[:, :], AF.Copy, scale=SCALE,
                  inc=s_kev)
            def _exp(s, c, j):
                J = 16 * c + j
                WG(s, s_snm, J + 1)
                WG(s, s_ptp, 2 * J)
                p.sop(s, "activation", psb_[:, :], sev[J % 2][:, :], AF.Exp,
                      bias=negMall[:, j:j + 1], accum_out=rsum[:, j:j + 1],
                      inc=s_pex)
            for rb in range(16):
                K = 16 * c + rb
                WG(s, s_smm, K + 1)
                if K >= 2:
                    WG(s, s_snm, K - 1)
                p.sop(s, "activation", sev[K % 2][:, :], psA[:, :], AF.Copy,
                      scale=1.0, inc=s_sev)
                if rb >= 1:
                    _exp(s, c, rb - 1)
            _exp(s, c, 15)

    @p.on("tensor")
    def _(te):
        WG(te, s_id, 16)
        for t in range(NT):
            WG(te, s_n0, t + 1)
            if t >= 1:
                WG(te, s_ev0, t)
            for fc in range(8):
                ins = te.transpose(ptr0[:, fc * 128:(fc + 1) * 128],
                                   xln[:, fc * 128:(fc + 1) * 128], id_bf[:, :])
            ins.then_inc(s_tp0.h, 1)
        WG(te, s_ev0, NT)
        WG(te, psem, 16 * 8)
        for c in range(NC):
            cb = c * 8 * 128
            WG(te, s_ares, 2 * c)
            for par in range(2):
                for fh in range(2):
                    for kc in range(8):
                        ins = te.matmul(
                            psB[par * 64:(par + 1) * 64, fh * 512:(fh + 1) * 512],
                            xlnT[:, cb + kc * 128 + par: cb + (kc + 1) * 128: 2],
                            wq_sb[:, kc * D + fh * 512:kc * D + (fh + 1) * 512],
                            start=(kc == 0), stop=(kc == 7),
                            tile_position=(0, par * 64))
            for fh in range(2):
                ins = te.matmul(psB[:, fh * 512:(fh + 1) * 512], ones_bf[:, :],
                                bq_sb[:, fh * 512:(fh + 1) * 512],
                                start=False, stop=True, skip_group_check=True)
            ins.then_inc(s_qmm.h, 1)
            if c >= 1:
                WG(te, s_sev, 16 * c)
            for par in range(2):
                for half in range(2):
                    for fh in range(2):
                        for kc in range(8):
                            ins = te.matmul(
                                psA[half * 64:(half + 1) * 64,
                                    par * D + fh * 512:par * D + (fh + 1) * 512],
                                xlnT[:, cb + kc * 128 + par: cb + (kc + 1) * 128: 2],
                                wk_sb[:, kc * D + fh * 512:kc * D + (fh + 1) * 512],
                                start=(kc == 0), stop=(kc == 7),
                                tile_position=(0, half * 64))
            for par in range(2):
                for fh in range(2):
                    ins = te.matmul(psA[:, par * D + fh * 512:par * D + (fh + 1) * 512],
                                    ones_bf[:, :], bk_sb[:, fh * 512:(fh + 1) * 512],
                                    start=False, stop=True, skip_group_check=True)
            ins.then_inc(s_kmm.h, 1)
            WG(te, s_qev, c + 1)
            for fc in range(8):
                for kc in range(8):
                    ins = te.matmul(psB[:, fc * 128:(fc + 1) * 128],
                                    wv_sb[:, kc * D + fc * 128:kc * D + (fc + 1) * 128],
                                    xlnT[:, cb + kc * 128:cb + (kc + 1) * 128],
                                    start=(kc == 0), stop=(kc == 7))
            ins.then_inc(s_vmm.h, 1)
            WG(te, s_kev, c + 1)
            if c >= 1:
                WG(te, s_afev, c)
            def _ptpv(te, c, j):
                J = 16 * c + j
                WG(te, s_pex, J + 1)
                for hh in range(2):
                    WG(te, s_ptev, 2 * J + hh)
                    for j8 in range(8):
                        jj = hh * 8 + j8
                        ins = te.transpose(psC[:, j8 * 128:(j8 + 1) * 128],
                                           psb_[:, jj * 128:(jj + 1) * 128], id_bf[:, :])
                    ins.then_inc(s_ptp.h, 1)
                WG(te, s_ptev, 2 * (J + 1))
                WG(te, s_vev, c + 1)
                if J >= 1:
                    WG(te, s_pvev, J)
                for k2c in range(16):
                    p2, f2c = k2c // 8, k2c % 8
                    ins = te.matmul(psB[:, 0:64],
                                    ptsb[:, k2c * 128:(k2c + 1) * 128],
                                    vt[:, f2c * 128 + p2:f2c * 128 + 128:2],
                                    start=(k2c == 0), stop=(k2c == 15))
                ins.then_inc(s_pvmm.h, 1)
            for rb in range(16):
                K = 16 * c + rb
                p1, f1c = rb // 8, rb % 8
                WG(te, s_sev, K)
                for p2 in range(2):
                    for fh in range(2):
                        ins = te.matmul(
                            psA[:, p2 * D + fh * 512:p2 * D + (fh + 1) * 512],
                            qall[p1 * 64:(p1 + 1) * 64, f1c * 128:(f1c + 1) * 128],
                            kdup[p1 * 64:(p1 + 1) * 64, p2 * D + fh * 512:p2 * D + (fh + 1) * 512],
                            start=True, stop=True)
                ins.then_inc(s_smm.h, 1)
                if rb >= 1:
                    _ptpv(te, c, rb - 1)
            _ptpv(te, c, 15)
            WG(te, s_rr2, 16 * (c + 1))
            WG(te, s_ptev, 32 * (c + 1))
            for fc in range(8):
                ins = te.transpose(ptr0[:, fc * 128:(fc + 1) * 128],
                                   a_tm[:, fc * 128:(fc + 1) * 128], id_bf[:, :])
            ins.then_inc(s_atp.h, 1)
            WG(te, s_afev, c + 1)
            WG(te, s_pvev, 16 * (c + 1))
            for h in range(2):
                for dc in range(8):
                    ins = te.matmul(psB[:, h * 512:(h + 1) * 512],
                                    a_fm[:, dc * 128:(dc + 1) * 128],
                                    wo_sb[:, dc * D + h * 512:dc * D + (h + 1) * 512],
                                    start=(dc == 0), stop=False)
                ins2 = te.matmul(psB[:, h * 512:(h + 1) * 512], ones_bf[:, :],
                                 bo_sb[:, h * 512:(h + 1) * 512],
                                 start=False, stop=True)
            ins2.then_inc(s_omm2.h, 1)
    s_tp0.v += NT


    s_id.v = 16
    psem.v = 128
    s_rr1[0].v = 128
    s_rr1[1].v = 128
    s_rr2.v = 256
    s_qmm.v = 16
    s_kmm.v = 16
    s_vmm.v = 16
    s_smm.v = 256
    s_ptp.v = 512
    s_pvmm.v = 256
    s_atp.v = 16
    s_omm2.v = 16
    return s_ares, 2 * NC


def attn_aps(nc):
    return {
        "wq_dev": nc.dram_tensor("wq_dev", [128, 8 * D], F32, kind="ExternalInput").ap(),
        "wk_dev": nc.dram_tensor("wk_dev", [128, 8 * D], F32, kind="ExternalInput").ap(),
        "wv_dev": nc.dram_tensor("wv_dev", [128, 8 * D], F32, kind="ExternalInput").ap(),
        "wo_dev": nc.dram_tensor("wo_dev", [128, 8 * D], F32, kind="ExternalInput").ap(),
        "bq_row": nc.dram_tensor("bq_row", [1, D], F32, kind="ExternalInput").ap(),
        "bk_row": nc.dram_tensor("bk_row", [1, D], F32, kind="ExternalInput").ap(),
        "bv_cols": nc.dram_tensor("bv_cols", [128, 8], F32, kind="ExternalInput").ap(),
        "bo_row": nc.dram_tensor("bo_row", [1, D], F32, kind="ExternalInput").ap(),
        "ascr": [nc.dram_tensor(f"ascr{i}", [128, D], BF16).ap() for i in range(2)],
    }


def build_attn(dbg=False):
    return _standalone(emit_attn, attn_aps)


def _stage_attn(inputs):
    c = np.ascontiguousarray
    def wdev(w):
        return c(w.reshape(8, 128, D).transpose(1, 0, 2).reshape(128, 8 * D))
    return {
        "wq_dev": wdev(inputs["L1_attn_Wq"]),
        "wk_dev": wdev(inputs["L1_attn_Wk"]),
        "wv_dev": wdev(inputs["L1_attn_Wv"]),
        "wo_dev": wdev(inputs["L1_attn_Wo"]),
        "bq_row": c(inputs["L1_attn_bq"].reshape(1, D)),
        "bk_row": c(inputs["L1_attn_bk"].reshape(1, D)),
        "bv_cols": c(inputs["L1_attn_bv"].reshape(8, 128).T.reshape(128, 8)),
        "bo_row": c(inputs["L1_attn_bo"].reshape(1, D)),
        "ident": np.eye(128, dtype=np.float32),
    }



MAGIC = 12582912.0


def emit_bitffn(p, A, sx, gate, wprep=True):
    NSL1, NSL2 = 8, 32
    NW1 = float(4096 * 1024)
    w1t = A["w1t_dev"]
    w2t = A["w2t_dev"]
    cw = A["cw_dev"]
    cb = A["cb_dev"]
    ident = A["ident"]
    w1q_scr = A["w1q_scr"]
    w2q_scr = A["w2q_scr"]
    dqscr = A["dqscr"]
    s2scr = A["s2scr"]
    mc_scr = A["mc_scr"]
    xqT = p.sb("xqT", [128, 8 * T], BF16)
    xln = p.sb("xln0", [128, D], BF16)
    f32t = p.sb("f32t", [128, D])
    hsil = p.sb("hsil", [128, 32 * 512], BF16)
    hq = hsil
    w1s = [p.sb(f"w1s{i}", [128, 1024], BF16) for i in range(2)]
    w2s = [p.sb(f"w2s{i}", [128, 4096], BF16) for i in range(2)]
    cvsrc = p.sb("cvsrc", [128, 514])
    cvb = p.sb("cvb", [128, 512])
    a1c = p.sb("a1c", [128, 8])
    a2c = p.sb("a2c", [128, 32])
    r12 = p.sb("r12", [128, 2])
    mc1 = p.sb("mc1", [128, 1])
    mc2 = p.sb("mc2", [128, 1])
    ws1c = p.sb("ws1c", [128, 1])
    ws2c = p.sb("ws2c", [128, 1])
    amaxall = p.sb("amaxall", [128, NT])
    amcall = p.sb("amcall", [128, NT])
    sxc = p.sb("sxc", [128, NT])
    dq1all = p.sb("dq1all", [128, NT])
    dq1cols = p.sb("dq1cols", [16, 128])
    dq1row = p.sb("dq1row", [1, T])
    amax2 = p.sb("amax2", [128, 4])
    rtmp4 = p.sb("rtmp4", [128, 4])
    amax2c = p.sb("amax2c", [128, 4])
    s2col = p.sb("s2col", [128, 4])
    dqh2 = p.sb("dqh2", [128, 4])
    s2cols4 = p.sb("s2cols4", [4, 128])
    s2row = p.sb("s2row", [1, 512])
    cw_sb = p.sb("cw_sb", [128, 96])
    cb_sb = p.sb("cb_sb", [128, 32])
    ones_bf = p.sb("ones_bf", [1, 128], BF16)
    onesf = p.sb("onesf", [128, 128])
    onesf1 = p.sb("onesf1", [1, 128])
    id_bf = p.sb("id_bf", [128, 128], BF16)
    id_f = p.sb("id_f", [128, 128])

    psum_h = p.ps("psum_h", [128, 514])
    psum_s = p.ps("psum_s", [128, 514])
    phb = [psum_h, psum_s]
    psum_w2 = p.ps("psum_w2", [128, 512])
    ptr0 = p.ps("ptr0", [128, 512], BF16)
    psum_c = p.ps("psum_c", [128, 128])

    psem = p.sem("psem", dma=True)
    s_id = p.sem("sid", dma=True)
    s_sl = [p.sem("sl0", dma=True), p.sem("sl1", dma=True)]
    s_abs = p.sem("abs")
    s_tot = p.sem("tot")
    s_wsc = p.sem("wsc")
    s_q3 = p.sem("q3")
    s_qst = [p.sem("qst0", dma=True), p.sem("qst1", dma=True)]
    s_n0 = p.sem("n0")
    s_amx = p.sem("amx")
    s_sxc = p.sem("sxc1")
    s_xq = p.sem("xq")
    s_tp0 = p.sem("tp0")
    s_ev0 = p.sem("ev0")
    s_dqt = p.sem("dqt")
    s_dqcv = p.sem("dqcv")
    s_dqb = p.sem("dqb", dma=True)
    s_w1l = [p.sem("w1l0", dma=True), p.sem("w1l1", dma=True)]
    s_w2l = [p.sem("w2l0", dma=True), p.sem("w2l1", dma=True)]
    s_dqr = p.sem("dqr")
    s_dqe = p.sem("dqe")
    s_cve = p.sem("cve")
    s_hmm = p.sem("hmm")
    s_cv = p.sem("cv")
    s_sil = p.sem("sil")
    s_htp = p.sem("htp")
    s_hmx = p.sem("hmx")
    s_s2c = p.sem("s2c")
    s_s2t = p.sem("s2t")
    s_s2cv = p.sem("s2cv")
    s_s2b = p.sem("s2b", dma=True)
    s_s2r = p.sem("s2r")
    s_s2e = p.sem("s2e")
    s_hq = p.sem("hq3")
    s_wmm = p.sem("wmm")
    s_oev = p.sem("oev")
    s_otp = p.sem("otp")
    s_ores = p.sem("ores")
    s_mcl = p.sem("mcl", dma=True)
    s_mcv = p.sem("mcv")

    rstd, negm, s_rstd = emit_ln(p, sx, gate, "ln")

    _w0, _w0cm = p.sb2("wsl0", [128, 4096])
    wslab = [_w0, _w0]
    qslab = [hsil[:, 0:4096], hsil[:, 4096:8192]]
    _w0cm.__exit__(None, None, None)
    dqrep = p.sb("dqrep", [128, 514])
    s2rep = p.sb("s2rep", [128, 512])
    otmp = p.sb("otmp", [128, 512], BF16)

    @p.on("sync")
    def _(sy):
        for i in range(NSL1 + NSL2 if wprep else 0):
            if i >= 1:
                WG(sy, s_abs, i)
            if i < NSL1:
                sy.dma_start(out=wslab[i % 2][:, :],
                             in_=w1t[i * 128:(i + 1) * 128, :]).then_inc(s_sl[i % 2].h, 16)
            else:
                j = i - NSL1
                sy.dma_start(out=wslab[i % 2][:, 0:1024],
                             in_=w2t[j * 128:(j + 1) * 128, :]).then_inc(s_sl[i % 2].h, 16)
        for i in range(NSL1 + NSL2 if wprep else 0):
            ii = NSL1 + NSL2 + i
            if i == 0:
                WG(sy, s_abs, 40)
            else:
                WG(sy, s_q3, i)
            if i < NSL1:
                sy.dma_start(out=wslab[ii % 2][:, :],
                             in_=w1t[i * 128:(i + 1) * 128, :]).then_inc(s_sl[ii % 2].h, 16)
            else:
                j = i - NSL1
                sy.dma_start(out=wslab[ii % 2][:, 0:1024],
                             in_=w2t[j * 128:(j + 1) * 128, :]).then_inc(s_sl[ii % 2].h, 16)
            WG(sy, s_q3, i + 1)
            if i < NSL1:
                dst = bass.AP(w1q_scr.tensor, i * 128,
                              [[1024, 128], [128 * 1024, 32], [1, 128]])
                sy.dma_start(out=dst, in_=qslab[i % 2][:, :]).then_inc(s_qst[i % 2].h, 16)
            else:
                j = i - NSL1
                dst = bass.AP(w2q_scr.tensor, j * 128,
                              [[4096, 128], [128 * 4096, 8], [1, 128]])
                sy.dma_start(out=dst, in_=qslab[i % 2][0:128, 0:1024]).then_inc(s_qst[i % 2].h, 16)
        WG(sy, s_dqcv, 1)
        sy.dma_start(out=dqscr, in_=dq1cols[:, :]).then_inc(s_dqb.h, 16)
        WG(sy, s_dqb, 16)
        sy.dma_start(out=dq1row[:, :],
                     in_=dqscr.rearrange("(one a) b -> one (a b)", one=1)).then_inc(s_dqb.h, 16)
        for tc in range(4):
            for fo in range(32):
                i = tc * 32 + fo
                if i >= 2:
                    WG(sy, s_hmm, i - 1)
                sy.dma_start(out=w1s[i % 2][:, :],
                             in_=w1q_scr[fo, :, :]).then_inc(s_w1l[i % 2].h, 16)
            WG(sy, s_s2cv, tc + 1)
            sy.dma_start(out=s2scr[tc, :, :], in_=s2cols4[:, :]).then_inc(s_s2b.h, 16)
            WG(sy, s_s2b, 32 * tc + 16)
            if tc >= 1:
                WG(sy, s_s2r, tc)
            sy.dma_start(out=s2row[:, :],
                         in_=s2scr[tc].rearrange("(one a) b -> one (a b)", one=1)).then_inc(s_s2b.h, 16)
            for dc in range(8):
                i = tc * 8 + dc
                if i >= 2:
                    WG(sy, s_wmm, i - 1)
                sy.dma_start(out=w2s[i % 2][:, :],
                             in_=w2q_scr[dc, :, :]).then_inc(s_w2l[i % 2].h, 16)
        if wprep:
            WG(sy, s_mcv, 1)
            sy.dma_start(out=mc_scr[0, :], in_=mc1[:, :]).then_inc(s_mcl.h, 16)
            sy.dma_start(out=mc_scr[1, :], in_=mc2[:, :]).then_inc(s_mcl.h, 16)

    @p.on("gpsimd")
    def _(g):
        g.dma_start(out=id_bf[:, :], in_=ident).then_inc(s_id.h, 16)
        g.dma_start(out=cw_sb[:, :], in_=cw).then_inc(psem.h, 16)
        g.dma_start(out=cb_sb[:, :], in_=cb).then_inc(psem.h, 16)
        g.dma_start(out=id_f[:, :], in_=ident).then_inc(psem.h, 16)
        if not wprep:
            g.dma_start(out=mc1[:, :], in_=mc_scr[0, :]).then_inc(s_mcl.h, 16)
            g.dma_start(out=mc2[:, :], in_=mc_scr[1, :]).then_inc(s_mcl.h, 16)

    @p.on("scalar")
    def _(s):
        for i in range(NSL1 + NSL2 if wprep else 0):
            WG(s, s_sl[i % 2], 16 * (i // 2 + 1))
            if i < NSL1:
                p.sop(s, "activation", qslab[i % 2][:, :], wslab[i % 2][:, :],
                      AF.Abs, accum_out=a1c[:, i:i + 1], inc=s_abs)
            else:
                j = i - NSL1
                p.sop(s, "activation", qslab[i % 2][0:128, 0:1024], wslab[i % 2][:, 0:1024],
                      AF.Abs, accum_out=a2c[:, j:j + 1], inc=s_abs)
        WG(s, psem, 48)
        for tc in range(4):
            for fo in range(32):
                i = tc * 32 + fo
                WG(s, s_cv, i + 1)
                if wprep and tc == 0 and fo == 0:
                    WG(s, s_qst[0], 16 * 20)
                    WG(s, s_qst[1], 16 * 20)
                if tc >= 1 and fo == 0:
                    WG(s, s_wmm, 8 * tc)
                p.sop(s, "activation", hsil[:, fo * 512:(fo + 1) * 512], cvb[:, :],
                      AF.Silu, bias=cb_sb[:, fo:fo + 1], inc=s_sil)
            for dc in range(8):
                i = tc * 8 + dc
                WG(s, s_wmm, i + 1)
                if i >= 1:
                    WG(s, s_otp, i)
                p.sop(s, "activation", otmp[:, :], psum_w2[:, :], AF.Copy, inc=s_oev)

    @p.on("tensor")
    def _(te):
        WG(te, s_id, 16)
        WG(te, psem, 48)
        if wprep:
            WG(te, s_wsc, 1)
            te.matmul(psum_c[:, 0:1], onesf[:, :], r12[:, 0:1], start=True, stop=True)
            te.matmul(psum_c[:, 1:2], onesf[:, :], r12[:, 1:2],
                      start=True, stop=True).then_inc(s_tot.h, 1)
        for t in range(NT):
            WG(te, s_xq, t + 1)
            for hh in range(2):
                if 2 * t + hh >= 1:
                    WG(te, s_ev0, 2 * t + hh)
                for fc4 in range(4):
                    fc = hh * 4 + fc4
                    ins = te.transpose(ptr0[:, fc4 * 128:(fc4 + 1) * 128],
                                       xln[:, fc * 128:(fc + 1) * 128], id_bf[:, :])
                ins.then_inc(s_tp0.h, 1)
        WG(te, s_sxc, 1)
        te.matmul(psum_c[0:16, 0:128], dq1all[:, :], id_f[:, :],
                  start=True, stop=True).then_inc(s_dqt.h, 1)
        WG(te, s_ev0, 2 * NT)
        WG(te, s_dqb, 32)
        for tc in range(4):
            lo = tc * 512 - 1
            lo0 = max(lo, 0)
            hi0 = min(lo + 513, T - 1)
            nv = hi0 - lo0 + 1
            WG(te, s_s2e, tc)
            WG(te, s_cve, 32 * tc)
            ins = te.matmul(psum_s[:, 0:512], onesf1[:, :],
                            dq1row[:, lo0:lo0 + 512], start=True, stop=True)
            ins = te.matmul(psum_s[:, 512:nv], onesf1[:, :],
                            dq1row[:, lo0 + 512:lo0 + nv], start=True, stop=True)
            ins.then_inc(s_dqr.h, 1)
            def _htp(te, tc, j):
                i2 = tc * 32 + j
                WG(te, s_sil, i2 + 1)
                if i2 >= 1:
                    WG(te, s_hmx, i2)
                if j == 0:
                    WG(te, s_ores, 32 * tc)
                for ts4 in range(4):
                    ins = te.transpose(ptr0[:, ts4 * 128:(ts4 + 1) * 128],
                                       hsil[:, j * 512 + ts4 * 128:j * 512 + (ts4 + 1) * 128],
                                       id_bf[:, :])
                ins.then_inc(s_htp.h, 1)
            for fo in range(32):
                i = tc * 32 + fo
                b = phb[fo % 2]
                WG(te, s_w1l[i % 2], 16 * (i // 2 + 1))
                if i >= 2:
                    WG(te, s_cve, i - 1)
                if fo == 1:
                    WG(te, s_dqe, tc + 1)
                for kc in range(8):
                    ins = te.matmul(b[:, 0:512],
                                    w1s[i % 2][:, kc * 128:(kc + 1) * 128],
                                    xqT[:, kc * T + lo0:kc * T + lo0 + 512],
                                    start=(kc == 0), stop=(kc == 7))
                for kc in range(8):
                    ins = te.matmul(b[:, 512:nv],
                                    w1s[i % 2][:, kc * 128:(kc + 1) * 128],
                                    xqT[:, kc * T + lo0 + 512:kc * T + lo0 + nv],
                                    start=(kc == 0), stop=(kc == 7))
                ins.then_inc(s_hmm.h, 1)
                if fo >= 1:
                    _htp(te, tc, fo - 1)
            _htp(te, tc, 31)
            WG(te, s_s2c, tc + 1)
            if tc >= 1:
                WG(te, s_s2cv, tc)
            te.matmul(psum_c[0:4, 0:128], s2col[:, :], id_f[:, :],
                      start=True, stop=True).then_inc(s_s2t.h, 1)
            WG(te, s_s2b, 32 * tc + 32)
            WG(te, s_dqe, tc + 1)
            WG(te, s_cve, 32 * (tc + 1))
            te.matmul(psum_s[:, 0:512], onesf1[:, :], s2row[:, :],
                      start=True, stop=True).then_inc(s_s2r.h, 1)
            for dc in range(8):
                i = tc * 8 + dc
                WG(te, s_w2l[i % 2], 16 * (i // 2 + 1))
                WG(te, s_hq, 32 * (tc + 1))
                if i >= 1:
                    WG(te, s_oev, i)
                for fi in range(32):
                    ins = te.matmul(psum_w2[:, :],
                                    w2s[i % 2][:, fi * 128:(fi + 1) * 128],
                                    hq[:, fi * 512:(fi + 1) * 512],
                                    start=(fi == 0), stop=(fi == 31))
                ins.then_inc(s_wmm.h, 1)
                WG(te, s_oev, i + 1)
                if i >= 1:
                    WG(te, s_ores, 4 * i)
                if dc == 0:
                    WG(te, s_hmx, 32 * (tc + 1))
                for ts4 in range(4):
                    ins = te.transpose(ptr0[:, ts4 * 128:(ts4 + 1) * 128],
                                       otmp[:, ts4 * 128:(ts4 + 1) * 128], id_bf[:, :])
                ins.then_inc(s_otp.h, 1)
    s_tp0.v += 2 * NT

    @p.on("vector")
    def _(v):
        p.vop(v, "memset", ones_bf[:, :], 1.0)
        p.vop(v, "memset", onesf[:, :], 1.0)
        p.vop(v, "memset", onesf1[:, :], 1.0)
        if wprep:
            WG(v, s_abs, 40)
            p.vop(v, "reduce_sum", r12[:, 0:1], a1c[:, :], AX)
            p.vop(v, "reduce_sum", r12[:, 1:2], a2c[:, :], AX, inc=s_wsc)
            WG(v, s_tot, 1)
            p.vop(v, "tensor_scalar", mc1[:, :], psum_c[:, 0:1], 1.0 / NW1, 1e-5,
                  ALU.mult, ALU.max)
            p.vop(v, "tensor_scalar", mc2[:, :], psum_c[:, 1:2], 1.0 / NW1, 1e-5,
                  ALU.mult, ALU.max, inc=s_mcv)
            p.vop(v, "reciprocal", ws1c[:, :], mc1[:, :])
            p.vop(v, "reciprocal", ws2c[:, :], mc2[:, :])
        else:
            WG(v, s_mcl, 32)
        for i in range(NSL1 + NSL2 if wprep else 0):
            ii = NSL1 + NSL2 + i
            WG(v, s_sl[ii % 2], 16 * (ii // 2 + 1))
            if i >= 2:
                WG(v, s_qst[i % 2], 16 * (i // 2))
            wcol = ws1c if i < NSL1 else ws2c
            ncols = 4096 if i < NSL1 else 1024
            p.vop(v, "tensor_scalar", wslab[ii % 2][:, 0:ncols], wslab[ii % 2][:, 0:ncols],
                  wcol[:, 0:1], MAGIC, ALU.mult, ALU.add)
            p.vop(v, "tensor_scalar", wslab[ii % 2][:, 0:ncols], wslab[ii % 2][:, 0:ncols],
                  MAGIC, -1.0, ALU.subtract, ALU.max)
            p.vop(v, "tensor_scalar", qslab[i % 2][0:128, 0:ncols], wslab[ii % 2][:, 0:ncols],
                  1.0, None, ALU.min, inc=s_q3)
        WG(v, s_rstd, 1)
        for t in range(NT):
            p.vop(v, "tensor_scalar", f32t[:, :], sx[:, t * D:(t + 1) * D],
                  negm[:, t:t + 1], rstd[:, t:t + 1], ALU.add, ALU.mult)
            p.vop(v, "tensor_reduce", amaxall[:, t:t + 1], f32t[:, :], AX, ALU.max,
                  apply_absolute_value=True, inc=s_amx)
        p.vop(v, "tensor_scalar", amcall[:, :], amaxall[:, :], 1e-5, None, ALU.max)
        p.vop(v, "reciprocal", sxc[:, :], amcall[:, :])
        p.vop(v, "tensor_scalar", sxc[:, :], sxc[:, :], 127.0, None, ALU.mult)
        p.vop(v, "tensor_scalar", dq1all[:, :], amcall[:, :], mc1[:, 0:1],
              1.0 / 127.0, ALU.mult, ALU.mult, inc=s_sxc)
        for t in range(NT):
            if t >= 1:
                WG(v, s_tp0, 2 * t)
            p.vop(v, "tensor_scalar", f32t[:, :], sx[:, t * D:(t + 1) * D],
                  negm[:, t:t + 1], rstd[:, t:t + 1], ALU.add, ALU.mult)
            p.vop(v, "tensor_scalar", f32t[:, :], f32t[:, :], sxc[:, t:t + 1],
                  MAGIC, ALU.mult, ALU.add)
            p.vop(v, "tensor_scalar", f32t[:, :], f32t[:, :], MAGIC, -128.0,
                  ALU.subtract, ALU.max)
            p.vop(v, "tensor_scalar", xln[:, :], f32t[:, :], 127.0, None,
                  ALU.min, inc=s_xq)
            for hh in range(2):
                WG(v, s_tp0, 2 * t + hh + 1)
                for fc4 in range(4):
                    fc = hh * 4 + fc4
                    p.vop(v, "tensor_copy",
                          xqT[:, fc * T + t * 128:fc * T + (t + 1) * 128],
                          ptr0[:, fc4 * 128:(fc4 + 1) * 128],
                          inc=(s_ev0 if fc4 == 3 else None))
        WG(v, s_dqt, 1)
        p.vop(v, "tensor_copy", dq1cols[:, :], psum_c[0:16, 0:128], inc=s_dqcv)
        for tc in range(4):
            lo = tc * 512 - 1
            lo0 = max(lo, 0)
            nv = min(lo + 513, T - 1) - lo0 + 1
            WG(v, s_dqr, tc + 1)
            p.vop(v, "tensor_copy", dqrep[:, 0:nv], psum_s[:, 0:nv], inc=s_dqe)
            off = -1 if tc == 0 else 0
            for fo in range(32):
                i = tc * 32 + fo
                WG(v, s_hmm, i + 1)
                p.vop(v, "tensor_tensor", cvsrc[:, 0:nv], phb[fo % 2][:, 0:nv],
                      dqrep[:, 0:nv], ALU.mult, inc=s_cve)
                p.vop(v, "tensor_scalar", cvb[:, :], cvsrc[:, 1 + off:513 + off],
                      cw_sb[:, fo * 3 + 1:fo * 3 + 2], None, ALU.mult)
                for k in (0, 2):
                    clo = k + off
                    rlo = max(0, -clo)
                    rhi = min(512, nv - clo)
                    p.vop(v, "scalar_tensor_tensor", cvb[:, rlo:rhi],
                          cvsrc[:, rlo + clo:rhi + clo],
                          cw_sb[:, fo * 3 + k:fo * 3 + k + 1], cvb[:, rlo:rhi],
                          ALU.mult, ALU.add, inc=(s_cv if k == 2 else None))
                WG(v, s_htp, i + 1)
                p.vop(v, "tensor_reduce", rtmp4[:, :],
                      ptr0[:, :].rearrange("p (a b) -> p a b", a=4), AX, ALU.max,
                      apply_absolute_value=True)
                if fo == 0:
                    p.vop(v, "tensor_copy", amax2[:, :], rtmp4[:, :], inc=s_hmx)
                else:
                    p.vop(v, "tensor_tensor", amax2[:, :], amax2[:, :],
                          rtmp4[:, :], ALU.max, inc=s_hmx)
            p.vop(v, "tensor_scalar", amax2c[:, :], amax2[:, :], 1e-5, None, ALU.max)
            p.vop(v, "reciprocal", s2col[:, :], amax2c[:, :])
            p.vop(v, "tensor_scalar", s2col[:, :], s2col[:, :], 127.0, None, ALU.mult)
            p.vop(v, "tensor_scalar", dqh2[:, :], amax2c[:, :], mc2[:, 0:1],
                  0.5 / 127.0, ALU.mult, ALU.mult, inc=s_s2c)
            WG(v, s_s2t, tc + 1)
            p.vop(v, "tensor_copy", s2cols4[:, :], psum_c[0:4, 0:128], inc=s_s2cv)
            WG(v, s_s2r, tc + 1)
            p.vop(v, "tensor_copy", s2rep[:, :], psum_s[:, 0:512], inc=s_s2e)
            for fo in range(32):
                i = tc * 32 + fo
                if tc >= 1 and fo == 0:
                    WG(v, s_wmm, 8 * tc)
                p.vop(v, "tensor_tensor", f32t[:, 0:512],
                      hsil[:, fo * 512:(fo + 1) * 512], s2rep[:, :], ALU.mult)
                p.vop(v, "tensor_scalar", f32t[:, 0:512], f32t[:, 0:512],
                      MAGIC, MAGIC, ALU.add, ALU.subtract)
                p.vop(v, "tensor_scalar", hq[:, fo * 512:(fo + 1) * 512],
                      f32t[:, 0:512], -128.0, 127.0, ALU.max, ALU.min, inc=s_hq)
            for dc in range(8):
                i = tc * 8 + dc
                WG(v, s_otp, i + 1)
                for ts4 in range(4):
                    tt = tc * 4 + ts4
                    p.vop(v, "scalar_tensor_tensor",
                          sx[:, tt * D + dc * 128:tt * D + (dc + 1) * 128],
                          ptr0[:, ts4 * 128:(ts4 + 1) * 128], dqh2[:, ts4:ts4 + 1],
                          sx[:, tt * D + dc * 128:tt * D + (dc + 1) * 128],
                          ALU.mult, ALU.add, inc=s_ores)


    s_id.v = 16
    psem.v = 48
    s_mcl.v = 32
    if wprep:
        s_sl[0].v = 640
        s_sl[1].v = 640
        s_qst[0].v = 320
        s_qst[1].v = 320
    s_dqb.v = 32
    s_w1l[0].v = 1024
    s_w1l[1].v = 1024
    s_w2l[0].v = 256
    s_w2l[1].v = 256
    s_s2b.v = 128
    if wprep:
        s_tot.v = 1
    s_dqt.v = 1
    s_dqr.v = 4
    s_hmm.v = 128
    s_htp.v = 128
    s_s2t.v = 4
    s_s2r.v = 4
    s_wmm.v = 32
    s_otp.v = 32
    return s_ores, 4 * 8 * 4


def bitffn_aps(nc):
    return {
        "w1t_dev": nc.dram_tensor("w1t_dev", [1024, 4096], F32, kind="ExternalInput").ap(),
        "w2t_dev": nc.dram_tensor("w2t_dev", [4096, 1024], F32, kind="ExternalInput").ap(),
        "cw_dev": nc.dram_tensor("cw_dev", [128, 96], F32, kind="ExternalInput").ap(),
        "cb_dev": nc.dram_tensor("cb_dev", [128, 32], F32, kind="ExternalInput").ap(),
        "w1q_scr": nc.dram_tensor("w1q_scr", [32, 128, 1024], BF16).ap(),
        "mc_scr": nc.dram_tensor("mc_scr", [2, 128], F32).ap(),
        "w2q_scr": nc.dram_tensor("w2q_scr", [8, 128, 4096], BF16).ap(),
        "dqscr": nc.dram_tensor("dqscr", [16, 128], F32).ap(),
        "s2scr": nc.dram_tensor("s2scr", [4, 4, 128], F32).ap(),
    }


def build_bitffn(dbg=False):
    return _standalone(emit_bitffn, bitffn_aps)


def _stage_bitffn(inputs):
    c = np.ascontiguousarray
    return {
        "w1t_dev": c(inputs["L1_ffn_w1"].T),
        "w2t_dev": c(inputs["L1_ffn_w2"].T),
        "cw_dev": c(inputs["L1_ffn_conv_w"].reshape(32, 128, 3).transpose(1, 0, 2).reshape(128, 96)),
        "cb_dev": c(inputs["L1_ffn_conv_b"].reshape(32, 128).T.reshape(128, 32)),
        "ident": np.eye(128, dtype=np.float32),
    }


def build_fused(nblocks=6, pool=True, start=0):
    nc = bass.Bass("TRN2", target_bir_lowering=False)
    x = nc.dram_tensor("x", [T, D], F32, kind="ExternalInput").ap()
    xo = nc.dram_tensor("xo", [T, D], F32, kind="ExternalOutput").ap()
    A = {}
    A.update(moe_aps(nc))
    A.update(mamba_aps(nc))
    A.update(attn_aps(nc))
    A.update(bitffn_aps(nc))
    A["ident"] = nc.dram_tensor("ident", [128, 128], F32, kind="ExternalInput").ap()
    with contextlib.ExitStack() as es:
        p = Prog(nc, es)
        if pool:
            p.pool = [es.enter_context(nc.semaphore(f"sp{i}")) for i in range(56)]
            p.pool_base = [0] * 56
        bar0 = p.sem("bar0")
        bar1 = p.sem("bar1")
        gdsem = p.sem("gdsem")
        sx = p.sb("sx", [128, NT * D])

        @p.on("sync")
        def _(sy):
            for t in range(NT):
                sy.dma_start(out=sx[:, t * D:(t + 1) * D],
                             in_=x[t * 128:(t + 1) * 128, :]).then_inc(gdsem.h, 16)

        bitffn_nw = lambda p, A, sx, gate: emit_bitffn(p, A, sx, gate, wprep=False)
        seq = [("L0fa", emit_moe), ("L0mx", emit_mamba), ("L0fb", emit_moe),
               ("L1fa", emit_bitffn), ("L1ax", emit_attn), ("L1fb", bitffn_nw)][start:start + nblocks]
        for k, (pref, em) in enumerate(seq):
            p.begin_block(pref)
            gate = (gdsem, 16 * NT) if k == 0 else (bar1, k)
            em(p, A, sx, gate)
            sems = p.end_block()
            emit_barrier(p, k, sems, bar0, bar1)

        @p.on("sync")
        def _(sy):
            WG(sy, bar1, len(seq))
            for t in range(NT):
                sy.dma_start(out=xo[t * 128:(t + 1) * 128, :],
                             in_=sx[:, t * D:(t + 1) * D]).then_inc(gdsem.h, 16)
            WG(sy, gdsem, 16 * NT + 16 * NT)

        p.finalize()
    return nc


def kernel(**inputs):
    x = np.ascontiguousarray(inputs["L0_attn_hidden_states"])
    W = {}
    W.update(_stage_moe(inputs))
    W.update(_stage_mamba(inputs))
    W.update(_stage_attn(inputs))
    W.update(_stage_bitffn(inputs))
    nc = build_fused()
    in_maps = [dict(W, x=np.ascontiguousarray(x[b])) for b in range(8)]
    res = run_bass_kernel_spmd(nc, in_maps, core_ids=list(range(8)))
    return np.stack([np.asarray(res.results[b]["xo"]) for b in range(8)], 0)
